# version 16
# baseline (speedup 1.0000x reference)
import os
import numpy as np
import ml_dtypes
from contextlib import ExitStack
import concourse.bass as bass
import concourse.mybir as mybir
from concourse.bass_utils import run_bass_kernel_spmd

F32 = mybir.dt.float32
BF16 = mybir.dt.bfloat16
AF = mybir.ActivationFunctionType
ALU = mybir.AluOpType
AXX = mybir.AxisListType.X

DEBUG_DUMP = [s for s in os.environ.get("MK_DEBUG_DUMP", "").split(",") if s]
PHASE_LIMIT = int(os.environ.get("MK_PHASE_LIMIT", "99"))

B, T, D = 16, 2048, 1024
NCORES = 8
SEQ = B // NCORES
M = SEQ * T
DN_HEADS, DN_D, DN_C = 4, 256, 64
DN_CHUNKS = T // DN_C
IN_DEPTH, IN_HEADS, IN_DH, WINDOW = 2, 8, 64, 512
INNER = IN_HEADS * IN_DH
HID, HID2 = 2730, 5460
NW = T // WINDOW
N_PAIR = SEQ * DN_HEADS
QK_SCALE = 8.0
KTD = D // 128
HIDP = 22 * 128
LNEPS = float(np.float32(1e-5))


class Sched:
    COMPUTE = ("T", "V", "S", "P", "Y")
    QUEUE_OF = {"Y": "QY", "S": "QS", "P": "QP"}
    POOL = 24

    def __init__(self):
        self.ops = []
        self.eng_ops = {e: [] for e in self.COMPUTE}
        self.trk_count = {t: 0 for t in ("T", "V", "S", "P", "Y")}
        self.q_count = {q: 0 for q in ("QY", "QS", "QP")}
        for q in ("QY", "QS", "QP"):
            for s in range(self.POOL):
                self.trk_count[(q, s)] = 0
        self.last_w = {}
        self.readers = {}

    def emit(self, eng, fn, reads=(), writes=(), dma=False, tag=""):
        if eng == "P" and not dma and not os.environ.get("MK_USE_GP"):
            eng = "V"
        if dma:
            q = self.QUEUE_OF[eng]
            slot = self.q_count[q] % self.POOL
            self.q_count[q] += 1
            tracker = (q, slot)
        else:
            tracker = eng
        seq = self.trk_count[tracker]
        self.trk_count[tracker] += 1
        deps = set()
        for r in reads:
            w = self.last_w.get(r)
            if w is not None:
                deps.add(w)
        for w_ in writes:
            lw = self.last_w.get(w_)
            if lw is not None:
                deps.add(lw)
            deps.update(self.readers.get(w_, ()))
        idx = len(self.ops)
        op = dict(idx=idx, eng=eng, fn=fn, tracker=tracker, seq=seq,
                  deps=sorted(deps), dma=dma, tag=tag)
        self.ops.append(op)
        self.eng_ops[eng].append(op)
        for r in reads:
            self.readers.setdefault(r, []).append(idx)
        for w_ in writes:
            self.last_w[w_] = idx
            self.readers[w_] = []
        return idx

    def link(self, new_key, old_keys):
        deps = []
        for k in old_keys:
            w = self.last_w.get(k)
            if w is not None:
                deps.append(w)
            deps.extend(self.readers.get(k, ()))
        self.readers.setdefault(new_key, []).extend(sorted(set(deps)))

    def barrier(self):
        lasts = {}
        for op in self.ops:
            lasts[op["tracker"]] = op["idx"]
        dep_idxs = sorted(lasts.values())
        for e in self.COMPUTE:
            seq = self.trk_count[e]
            self.trk_count[e] += 1
            idx = len(self.ops)
            op = dict(idx=idx, eng=e, fn=(lambda E: E.nop()), tracker=e,
                      seq=seq, deps=list(dep_idxs), dma=False, tag="barrier")
            self.ops.append(op)
            self.eng_ops[e].append(op)

    def resolve(self):
        clocks = {}
        cur = {e: {} for e in self.COMPUTE}
        self.need_inc = set()
        self.op_waits = {}
        for op in self.ops:
            e = op["eng"]
            clk = cur[e]
            waits = []
            changed = False
            for d in op["deps"]:
                p = self.ops[d]
                pt, ps = p["tracker"], p["seq"]
                implicit = (not p["dma"]) and (not op["dma"]) and pt == e == "T"
                if implicit:
                    for t, s in clocks[d].items():
                        if clk.get(t, -1) < s:
                            if not changed:
                                clk = dict(clk); changed = True
                            clk[t] = s
                    continue
                if clk.get(pt, -1) >= ps:
                    continue
                waits.append(d)
                if not changed:
                    clk = dict(clk); changed = True
                clk[pt] = ps
                self.need_inc.add(d)
                for t, s in clocks[d].items():
                    if clk.get(t, -1) < s:
                        clk[t] = s
            cur[e] = clk
            done = dict(clk)
            done[op["tracker"]] = op["seq"]
            clocks[op["idx"]] = done
            self.op_waits[op["idx"]] = waits
        for op in self.ops:
            if op["dma"]:
                self.need_inc.add(op["idx"])
        self.incnum = {}
        cnt = {t: 0 for t in self.trk_count}
        for op in self.ops:
            if op["idx"] in self.need_inc:
                cnt[op["tracker"]] += 1
                self.incnum[op["idx"]] = cnt[op["tracker"]]
        self.final_counts = dict(cnt)

    def materialize(self, nc, es):
        self.resolve()
        sems = {t: es.enter_context(nc.semaphore(f"sem_{i}"))
                for i, t in enumerate(self.trk_count)}
        block = es.enter_context(nc.Block())

        def run_stream(E, name):
            for op in self.eng_ops[name]:
                for d in self.op_waits[op["idx"]]:
                    p = self.ops[d]
                    val = self.incnum[d] * (16 if p["dma"] else 1)
                    E.wait_ge(sems[p["tracker"]], val)
                if op["dma"] and op["seq"] > 0:
                    E.wait_ge(sems[op["tracker"]], 16 * op["seq"])
                try:
                    inst = op["fn"](E)
                except Exception:
                    print(f"EMIT FAIL at op {op['idx']} eng={op['eng']} "
                          f"tag={op['tag']}")
                    raise
                if op["idx"] in self.need_inc:
                    inst.then_inc(sems[op["tracker"]], 16 if op["dma"] else 1)

        block.tensor(lambda E: run_stream(E, "T"))
        block.vector(lambda E: run_stream(E, "V"))
        block.scalar(lambda E: run_stream(E, "S"))
        block.gpsimd(lambda E: run_stream(E, "P"))

        def sync_fn(E):
            run_stream(E, "Y")
            for t, n in self.final_counts.items():
                if isinstance(t, tuple) and n > 0:
                    E.wait_ge(sems[t], 16 * n)
        block.sync(sync_fn)


class Arena:
    def __init__(self, nc):
        self.nc = nc
        self.off = (nc.sbuf_base + 31) & ~31
        self.limit = nc.sbuf_top - 64
        self.marks = []
        self.n = 0

    def alloc(self, name, shape, dtype):
        esz = 2 if dtype == BF16 else 4
        per_part = int(np.prod(shape[1:])) * esz
        off = (self.off + 31) & ~31
        assert off + per_part <= self.limit, \
            f"SBUF overflow: {name} needs {per_part}B at {off} (limit {self.limit})"
        self.n += 1
        t = self.nc.alloc_sbuf_tensor_at(f"{name}_{self.n}", list(shape), dtype,
                                         offset=off)
        self.off = off + per_part
        self.offsets = getattr(self, "offsets", {})
        self.offsets[name] = off
        return t.ap()

    def mark(self):
        self.marks.append(self.off)

    def reset(self):
        self.off = self.marks.pop()

    def alloc_at(self, name, shape, dtype, offset):
        self.n += 1
        t = self.nc.alloc_sbuf_tensor_at(f"{name}_{self.n}", list(shape), dtype,
                                         offset=offset)
        return t.ap()


def _host_constants():
    c = {}
    c["eye128"] = np.eye(128, dtype=ml_dtypes.bfloat16)
    c["eye64b"] = np.eye(DN_C, dtype=ml_dtypes.bfloat16)
    c["ones_b"] = np.ones((128, 128), dtype=ml_dtypes.bfloat16)
    c["ones_f"] = np.ones((128, 128), dtype=np.float32)
    p = np.arange(DN_C)
    diffs = np.maximum(p[:, None] - p[None, :], 0).astype(np.float32)
    c["diffs64"] = diffs
    c["diffs64T"] = np.ascontiguousarray(diffs.T)
    strict = np.tril(np.ones((DN_C, DN_C), np.float32), -1)
    c["strictlow_neg"] = -strict
    c["strictup_neg"] = -np.ascontiguousarray(strict.T)
    c["strictup_pos"] = np.ascontiguousarray(strict.T)
    c["pcol"] = p.astype(np.float32).reshape(DN_C, 1)
    c["ptailcol"] = (DN_C - 1 - p).astype(np.float32).reshape(DN_C, 1)
    c["prow"] = p.astype(np.float32).reshape(1, DN_C)
    r = np.arange(128)[:, None]
    q = np.arange(128)[None, :]
    NEG = np.float32(-30000.0)
    c["maskA"] = np.where(q > r, NEG, 0.0).astype(np.float32)
    c["maskB"] = np.where(q < r, NEG, 0.0).astype(np.float32)
    half = np.arange(0, IN_DH, 2, dtype=np.float32)
    inv_freq = 1.0 / 10000.0 ** (half / IN_DH)
    t = np.arange(T, dtype=np.float32)
    freqs = t[:, None] * inv_freq[None, :]
    freqs = np.concatenate([freqs, freqs], axis=-1)
    base = (half + 0.4 * IN_DH) / (1.4 * IN_DH)
    power = (t - T // 2) / float(WINDOW // 2)
    scale = base[None, :] ** power[:, None]
    scale = np.concatenate([scale, scale], axis=-1)
    cos, sin = np.cos(freqs), np.sin(freqs)
    sgn = np.where(np.arange(IN_DH) < IN_DH // 2, -1.0, 1.0).astype(np.float32)
    cs_q = (cos * scale).T.astype(np.float32)
    ss_q = (sin * scale).T.astype(np.float32) * sgn[:, None]
    cs_k = (cos / scale).T.astype(np.float32)
    ss_k = (sin / scale).T.astype(np.float32) * sgn[:, None]
    c["cs_q"] = np.tile(cs_q * QK_SCALE, (2, 1))
    c["ss_q"] = np.tile(ss_q * QK_SCALE, (2, 1))
    c["cs_k"] = np.tile(cs_k, (2, 1))
    c["ss_k"] = np.tile(ss_k, (2, 1))
    P1 = np.zeros((64, 64), np.float32)
    for d_ in range(64):
        P1[(d_ + 32) % 64, d_] = 1.0
    Psw = np.zeros((128, 128), np.float32)
    Psw[:64, :64] = P1
    Psw[64:, 64:] = P1
    c["Pswap"] = Psw.astype(ml_dtypes.bfloat16)
    ind = np.zeros((KTD, 128, DN_HEADS), np.float32)
    sel = np.zeros((KTD, DN_HEADS, 128), np.float32)
    for kt in range(KTD):
        ind[kt, :, kt // 2] = 1.0
        sel[kt, kt // 2, :] = 1.0
    c["ind_dn"] = ind.transpose(1, 0, 2).reshape(128, KTD * DN_HEADS).astype(ml_dtypes.bfloat16)
    c["sel_dn"] = sel.transpose(1, 0, 2).reshape(DN_HEADS, KTD * 128).astype(np.float32)
    ind2 = np.zeros((4, 128, IN_HEADS), np.float32)
    sel2 = np.zeros((4, IN_HEADS, 128), np.float32)
    for kt in range(4):
        ind2[kt, 0:64, 2 * kt] = 1.0
        ind2[kt, 64:128, 2 * kt + 1] = 1.0
        sel2[kt, 2 * kt, 0:64] = 1.0
        sel2[kt, 2 * kt + 1, 64:128] = 1.0
    c["ind_in"] = ind2.transpose(1, 0, 2).reshape(128, 4 * IN_HEADS).astype(ml_dtypes.bfloat16)
    c["sel_in"] = sel2.transpose(1, 0, 2).reshape(IN_HEADS, 4 * 128).astype(np.float32)
    return c


CONSTS = _host_constants()


def build_program():
    nc = bass.Bass("TRN2", target_bir_lowering=False, debug=False)
    sc = Sched()
    es = ExitStack()
    es.__enter__()

    H = {}
    H["x0"] = nc.dram_tensor("x0", [D, M], F32, kind="ExternalInput").ap()
    H["Ww"] = nc.dram_tensor("Ww", [D, D], F32, kind="ExternalInput").ap()
    H["Wr"] = nc.dram_tensor("Wr", [D, D], F32, kind="ExternalInput").ap()
    H["decay"] = nc.dram_tensor("decay", [1, DN_HEADS], F32, kind="ExternalInput").ap()
    H["log_alpha"] = nc.dram_tensor("log_alpha", [1, DN_HEADS], F32, kind="ExternalInput").ap()
    H["wqkv"] = nc.dram_tensor("wqkv", [IN_DEPTH * D, 3 * INNER], F32, kind="ExternalInput").ap()
    H["q_scale"] = nc.dram_tensor("q_scale", [IN_DEPTH, IN_DH], F32, kind="ExternalInput").ap()
    H["k_scale"] = nc.dram_tensor("k_scale", [IN_DEPTH, IN_DH], F32, kind="ExternalInput").ap()
    H["wo"] = nc.dram_tensor("wo", [IN_DEPTH * INNER, D], F32, kind="ExternalInput").ap()
    H["ln_g"] = nc.dram_tensor("ln_g", [IN_DEPTH, D], F32, kind="ExternalInput").ap()
    H["ln_b"] = nc.dram_tensor("ln_b", [IN_DEPTH, D], F32, kind="ExternalInput").ap()
    H["w1"] = nc.dram_tensor("w1", [IN_DEPTH * D, HID2], F32, kind="ExternalInput").ap()
    H["b1"] = nc.dram_tensor("b1", [IN_DEPTH, HID2], F32, kind="ExternalInput").ap()
    H["w2"] = nc.dram_tensor("w2", [IN_DEPTH * HID, D], F32, kind="ExternalInput").ap()
    H["b2"] = nc.dram_tensor("b2", [IN_DEPTH, D], F32, kind="ExternalInput").ap()
    CK = {}
    for k, v in CONSTS.items():
        dt = BF16 if v.dtype == ml_dtypes.bfloat16 else F32
        CK[k] = nc.dram_tensor("c_" + k, list(v.shape), dt, kind="ExternalInput").ap()
    out_h = nc.dram_tensor("out", [D, M], F32, kind="ExternalOutput").ap()

    v_hbm = nc.dram_tensor("v_hbm", [M, D], BF16).ap()
    ln1_hbm = nc.dram_tensor("ln1_hbm", [D, M], F32).ap()
    a_hbm = nc.dram_tensor("a_hbm", [D, M], F32).ap()
    x1_hbm = nc.dram_tensor("x1_hbm", [D, M], F32).ap()
    xin_hbm = nc.dram_tensor("xin_hbm", [D, M], F32).ap()
    wqkv_bf = nc.dram_tensor("wqkv_bf", [IN_DEPTH * 12 * 128, KTD * 128], BF16).ap()
    wo_bf = nc.dram_tensor("wo_bf", [IN_DEPTH * KTD * 128, 4 * 128], BF16).ap()
    w1_bf = nc.dram_tensor("w1_bf", [IN_DEPTH * 2 * 11 * 128, KTD * 256], BF16).ap()
    w2_bf = nc.dram_tensor("w2_bf", [IN_DEPTH * KTD * 128, 22 * 128], BF16).ap()

    dbg = {}
    for name in DEBUG_DUMP:
        shape, dt = {
            "wwt": ([D, D], BF16),
            "ln1": ([D, M], F32), "v": ([M, D], BF16), "rk": ([D, M], BF16),
            "o": ([D, M], BF16), "a": ([D, M], F32), "x1": ([D, M], F32),
            "xin": ([D, M], F32), "xwo": ([D, M], F32), "xffn": ([D, M], F32),
            "qf": ([INNER, M], BF16), "kf": ([INNER, M], BF16),
            "vv": ([INNER, M], BF16), "ot": ([INNER, M], BF16),
            "h2": ([D, M], BF16),
        }[name]
        dbg[name] = nc.dram_tensor("dbg_" + name, shape, dt,
                                   kind="ExternalOutput").ap()

    ar = Arena(nc)

    eye = ar.alloc("eye", [128, 128], BF16)
    onesb = ar.alloc("onesb", [128, 128], BF16)
    onesf = ar.alloc("onesf", [128, 128], F32)
    C = {}

    def load_const(name, shape, dtype):
        t = ar.alloc("c" + name, shape, dtype)
        sc.emit("Y", (lambda E, d=t, s=CK[name]: E.dma_start(out=d, in_=s)),
                writes=[("c", name)], dma=True, tag="const")
        C[name] = t
        return t

    sc.emit("Y", lambda E: E.dma_start(out=eye, in_=CK["eye128"]),
            writes=[("c", "eye")], dma=True)
    sc.emit("Y", lambda E: E.dma_start(out=onesb, in_=CK["ones_b"]),
            writes=[("c", "ones")], dma=True)
    sc.emit("Y", lambda E: E.dma_start(out=onesf, in_=CK["ones_f"]),
            writes=[("c", "onesf")], dma=True)
    for nm in ("eye64b", "diffs64", "diffs64T", "strictlow_neg", "strictup_neg",
               "strictup_pos", "pcol", "ptailcol", "prow", "ind_dn", "sel_dn",
               "Pswap", "ind_in", "sel_in", "maskA", "maskB"):
        load_const(nm, list(CONSTS[nm].shape), BF16 if CONSTS[nm].dtype == ml_dtypes.bfloat16 else F32)

    PSA = es.enter_context(nc.psum_tensor("PSA", [128, 2048], F32)).ap()
    PSB = es.enter_context(nc.psum_tensor("PSB", [128, 2048], F32)).ap()

    def bankA(i):
        return ("PSA", i)

    def bankB(i):
        return ("PSB", i)

    def T_mm(out, lhsT, rhs, st, sp, reads, wbanks, tag="mm", tp=None, sgc=False):
        sc.emit("T", (lambda E, o=out, l=lhsT, r=rhs, a=st, b=sp, t=tp, g=sgc:
                      E.matmul(o, l, r, start=a, stop=b, tile_position=t,
                               skip_group_check=g)),
                reads=list(reads), writes=list(wbanks), tag=tag)

    def T_tr(out, in_, ident, reads, wbanks, tag="tr"):
        sc.emit("T", (lambda E, o=out, i=in_, d=ident: E.transpose(o, i, d)),
                reads=list(reads), writes=list(wbanks), tag=tag)

    def S_act(out, in_, func, reads, writes, bias=0.0, scale=1.0, tag="act"):
        sc.emit("S", (lambda E, o=out, i=in_, f=func, b=bias, s=scale:
                      E.activation(out=o, in_=i, func=f, bias=b, scale=s)),
                reads=list(reads), writes=list(writes), tag=tag)

    def V_tt(out, in0, in1, op, reads, writes, eng="V", tag="tt"):
        sc.emit(eng, (lambda E, o=out, a=in0, b=in1, p=op:
                      E.tensor_tensor(o, a, b, op=p)),
                reads=list(reads), writes=list(writes), tag=tag)

    def V_stt(out, in0, scalar, in1, op0, op1, reads, writes, tag="stt"):
        sc.emit("V", (lambda E, o=out, a=in0, s=scalar, b=in1, p0=op0, p1=op1:
                      E.scalar_tensor_tensor(o, a, s, b, op0=p0, op1=p1)),
                reads=list(reads), writes=list(writes), tag=tag)

    def V_ts(out, in0, s1, op0, reads, writes, s2=None, op1=ALU.bypass, eng="V", tag="ts"):
        sc.emit(eng, (lambda E, o=out, a=in0, x1=s1, x2=s2, p0=op0, p1=op1:
                      E.tensor_scalar(o, a, x1, x2, op0=p0, op1=p1)),
                reads=list(reads), writes=list(writes), tag=tag)

    def DMA(eng, out, in_, reads, writes, tag="dma"):
        sc.emit(eng, (lambda E, o=out, i=in_:
                      E.dma_start(out=o, in_=i, allow_slow_non_contiguous=True)),
                reads=list(reads), writes=list(writes), dma=True, tag=tag)

    def V_copy(out, in_, reads, writes, eng="V", tag="cp"):
        sc.emit(eng, (lambda E, o=out, i=in_: E.tensor_copy(o, i)),
                reads=list(reads), writes=list(writes), tag=tag)

    def V_red(out, in_, op, reads, writes, tag="red", absval=False):
        sc.emit("V", (lambda E, o=out, i=in_, p=op, a=absval:
                      E.tensor_reduce(o, i, axis=AXX, op=p,
                                      apply_absolute_value=a)),
                reads=list(reads), writes=list(writes), tag=tag)

    def memset(ap, val, reads, writes, eng="V"):
        sc.emit(eng, (lambda E, a=ap, v=val: E.memset(a, v)),
                reads=list(reads), writes=list(writes), tag="memset")

    ztile = ar.alloc("ztile", [128, D], BF16)
    memset(ztile, 0.0, [], [("ztile",)])
    for l in range(IN_DEPTH if not os.environ.get("MK_SKIP_IMG") else 0):
        for mt in range(12):
            tb = (l * 12 + mt) * 128
            DMA("P", wqkv_bf[tb:tb + 128, :].rearrange("p (a c) -> p a c", a=KTD),
                H["wqkv"][l * D:(l + 1) * D, mt * 128:(mt + 1) * 128]
                .rearrange("(a p) c -> p a c", p=128),
                [], [("wqi", l, mt)], tag="img_wq")
        for mt in range(KTD):
            tb = (l * KTD + mt) * 128
            DMA("P", wo_bf[tb:tb + 128, :].rearrange("p (a c) -> p a c", a=4),
                H["wo"][l * INNER:(l + 1) * INNER, mt * 128:(mt + 1) * 128]
                .rearrange("(a p) c -> p a c", p=128),
                [], [("woi", l, mt)], tag="img_wo")
        for ag in range(2):
            off = ag * HID
            for grp in range(11):
                tb = ((l * 2 + ag) * 11 + grp) * 128
                gw = min(256, HID - grp * 256)
                if gw == 256:
                    DMA("P", w1_bf[tb:tb + 128, :]
                        .rearrange("p (a c) -> p a c", a=KTD),
                        H["w1"][l * D:(l + 1) * D,
                                off + grp * 256:off + grp * 256 + 256]
                        .rearrange("(a p) c -> p a c", p=128),
                        [], [("w1i", l, ag, grp)], tag="img_w1")
                else:
                    for kt in range(KTD):
                        DMA("P", w1_bf[tb:tb + 128, kt * 256:kt * 256 + gw],
                            H["w1"][l * D + kt * 128:l * D + kt * 128 + 128,
                                    off + grp * 256:off + grp * 256 + gw],
                            [], [("w1i", l, ag, grp)], tag="img_w1t")
                        DMA("Y", w1_bf[tb:tb + 128, kt * 256 + gw:(kt + 1) * 256],
                            ztile[:, 0:256 - gw], [("ztile",)],
                            [("w1i", l, ag, grp)], tag="img_w1z")
        for mo in range(KTD):
            tb = (l * KTD + mo) * 128
            DMA("P", w2_bf[tb:tb + 128, 0:21 * 128]
                .rearrange("p (a c) -> p a c", a=21),
                H["w2"][l * HID:l * HID + 2688, mo * 128:(mo + 1) * 128]
                .rearrange("(a p) c -> p a c", p=128),
                [], [("w2i", l, mo)], tag="img_w2")
            DMA("P", w2_bf[tb:tb + 42, 21 * 128:22 * 128],
                H["w2"][l * HID + 2688:l * HID + 2730, mo * 128:(mo + 1) * 128],
                [], [("w2i", l, mo)], tag="img_w2t")
            DMA("Y", w2_bf[tb + 42:tb + 128, 21 * 128:22 * 128],
                ztile[0:86, 0:128], [("ztile",)], [("w2i", l, mo)], tag="img_w2z")

    grow = ar.alloc("grow", [1, 512], F32)
    decay_sb = ar.alloc("decay_sb", [1, 128], F32)
    lalpha_sb = ar.alloc("lalpha_sb", [1, 128], F32)
    DMA("Y", decay_sb[:, 0:DN_HEADS], H["decay"], [], [("dec",)])
    DMA("Y", decay_sb[:, DN_HEADS:2 * DN_HEADS], H["decay"], [], [("dec",)])
    DMA("Y", lalpha_sb[:, 0:DN_HEADS], H["log_alpha"], [], [("lal",)])
    DMA("Y", lalpha_sb[:, DN_HEADS:2 * DN_HEADS], H["log_alpha"], [], [("lal",)])
    S_act(grow[:, 384:512], decay_sb, AF.Exp, [("dec",)], [("grow", 3)], scale=-1.0)
    V_ts(grow[:, 384:512], grow[:, 384:512], 1.0, ALU.add, [("grow", 3)], [("grow", 3)])
    S_act(grow[:, 128:256], grow[:, 384:512], AF.Ln, [("grow", 3)], [("grow", 1)],
          scale=1.0)
    V_ts(grow[:, 128:256], grow[:, 128:256], -1.0, ALU.mult, [("grow", 1)], [("grow", 1)])
    S_act(grow[:, 0:128], grow[:, 128:256], AF.Exp, [("grow", 1)], [("grow", 0)])
    S_act(grow[:, 256:384], lalpha_sb, AF.Exp, [("lal",)], [("grow", 2)])

    lg64 = [ar.alloc(f"lg64_{h}", [64, 1], F32) for h in range(DN_HEADS)]
    lg128 = [ar.alloc(f"lg128_{h}", [128, 1], F32) for h in range(DN_HEADS)]
    gp_col = [ar.alloc(f"gp_{h}", [64, 1], F32) for h in range(DN_HEADS)]
    gtail_col = [ar.alloc(f"gt_{h}", [64, 1], F32) for h in range(DN_HEADS)]
    gC_col = [ar.alloc(f"gC_{h}", [128, 1], F32) for h in range(DN_HEADS)]
    al_col = [ar.alloc(f"al_{h}", [128, 1], F32) for h in range(DN_HEADS)]
    gtile = [ar.alloc(f"gtile_{h}", [128, 64], BF16) for h in range(DN_HEADS)]
    mlow_neg = [ar.alloc(f"mlo_{h}", [64, 64], BF16) for h in range(DN_HEADS)]
    mup_neg = [ar.alloc(f"mun_{h}", [64, 64], BF16) for h in range(DN_HEADS)]
    mup_pos = [ar.alloc(f"mup_{h}", [64, 64], BF16) for h in range(DN_HEADS)]

    def emit_dn_consts():
        etmp = ar.alloc("etmp", [64, 64], F32)
        rtmp = ar.alloc("rtmp", [1, 64], F32)
        for h in range(DN_HEADS):
            if os.environ.get("MK_SERIAL"):
                sc.barrier()
            T_mm(PSA[0:128, 0:1], onesf[0:1, 0:128], grow[:, 128 + h:129 + h],
                 True, True, [("c", "onesf"), ("grow", 1)], [bankA(0)], tag="lgbc")
            V_copy(lg128[h], PSA[0:128, 0:1], [bankA(0)], [("lg128", h)])
            V_copy(lg64[h], PSA[0:64, 0:1], [bankA(0)], [("lg64", h)])
            T_mm(PSA[0:128, 512:513], onesf[0:1, 0:128], grow[:, 256 + h:257 + h],
                 True, True, [("c", "onesf"), ("grow", 2)], [bankA(1)], tag="albc")
            V_copy(al_col[h], PSA[0:128, 512:513], [bankA(1)], [("al", h)])
            S_act(gp_col[h], C["pcol"][0:64, :], AF.Exp, [("c", "pcol"), ("lg64", h)],
                  [("gp", h)], scale=lg64[h])
            S_act(gtail_col[h], C["ptailcol"][0:64, :], AF.Exp,
                  [("c", "ptailcol"), ("lg64", h)], [("gt", h)], scale=lg64[h])
            sc.emit("S", (lambda E, o=gC_col[h], i=lg128[h]:
                          E.activation(out=o, in_=i, func=AF.Exp, scale=float(DN_C))),
                    reads=[("lg128", h)], writes=[("gC", h)])
            V_ts(rtmp, C["prow"][0:1, 0:64], grow[:, 128 + h:129 + h], ALU.mult,
                 [("c", "prow"), ("grow", 1)], [("rtmp",)])
            S_act(rtmp, rtmp, AF.Exp, [("rtmp",)], [("rtmp",)])
            T_mm(PSA[0:128, 1024:1088], onesf[0:1, 0:128], rtmp[0:1, 0:64],
                 True, True, [("c", "onesf"), ("rtmp",)], [bankA(2)], tag="gtile")
            S_act(gtile[h], PSA[0:128, 1024:1088], AF.Copy, [bankA(2)], [("gtile", h)])
            S_act(etmp, C["diffs64"][0:64, :], AF.Exp, [("c", "diffs64"), ("lg64", h)],
                  [("etmp",)], scale=lg64[h])
            V_tt(mlow_neg[h], etmp, C["strictlow_neg"][0:64, :], ALU.mult,
                 [("etmp",), ("c", "strictlow_neg")], [("mlo", h)])
            S_act(etmp, C["diffs64T"][0:64, :], AF.Exp, [("c", "diffs64T"), ("lg64", h)],
                  [("etmp",)], scale=lg64[h])
            V_tt(mup_neg[h], etmp, C["strictup_neg"][0:64, :], ALU.mult,
                 [("etmp",), ("c", "strictup_neg")], [("mun", h)])
            V_tt(mup_pos[h], etmp, C["strictup_pos"][0:64, :], ALU.mult,
                 [("etmp",), ("c", "strictup_pos")], [("mup", h)])

    emit_dn_consts()
    if os.environ.get("MK_SERIAL"):
        sc.barrier()
    if os.environ.get("MK_DEBUG_DN"):
        cdbg = nc.dram_tensor("cdbg", [128, 1400], F32, kind="ExternalOutput").ap()
        for h in range(DN_HEADS):
            b = h * 200
            DMA("S", cdbg[0:64, b:b + 64].bitcast(BF16)[:, 0:64], mlow_neg[h],
                [("mlo", h)], [("cdbg", h, 0)])
            DMA("S", cdbg[0:128, b + 64:b + 96].bitcast(BF16)[:, 0:64], gtile[h],
                [("gtile", h)], [("cdbg", h, 1)])
            DMA("S", cdbg[0:64, b + 96:b + 97], lg64[h], [("lg64", h)], [("cdbg", h, 2)])
            DMA("S", cdbg[0:64, b + 97:b + 98], gp_col[h], [("gp", h)], [("cdbg", h, 3)])
            DMA("S", cdbg[0:64, b + 98:b + 99], gtail_col[h], [("gt", h)], [("cdbg", h, 4)])
            DMA("S", cdbg[0:128, b + 99:b + 100], gC_col[h], [("gC", h)], [("cdbg", h, 5)])
            DMA("S", cdbg[0:128, b + 100:b + 101], al_col[h], [("al", h)], [("cdbg", h, 6)])
        DMA("S", cdbg[0:1, 1000:1400].rearrange("a (b c) -> a b c", b=1), grow[:, 0:400].rearrange("a (b c) -> a b c", b=1), [("grow", 0), ("grow", 1), ("grow", 2)],
            [("cdbg", "grow")])

    o_hbm = nc.dram_tensor("o_hbm", [D, M], BF16).ap()
    ar.mark()
    rk = ar.alloc("rk", [128, KTD, M], BF16)

    def emit_wT(dst, src_h, dkey):
        wtile = ar.alloc("wtile" + dkey, [128, D], F32)
        wtb = ar.alloc("wtb" + dkey, [128, D], BF16)
        for ot in range(KTD):
            DMA("Y", wtile, src_h[ot * 128:(ot + 1) * 128, :], [],
                [(dkey, "wtile")], tag="wT")
            S_act(wtb, wtile, AF.Copy, [(dkey, "wtile")], [(dkey, "wtb")])
            for it in range(KTD):
                pb = PSA[:, (it % 4) * 512:(it % 4) * 512 + 128].bitcast(BF16)[:, 0:128]
                T_tr(pb, wtb[:, it * 128:(it + 1) * 128], eye,
                     [(dkey, "wtb"), ("c", "eye")], [bankA(it % 4)])
                S_act(dst[:, it, ot * 128:(ot + 1) * 128], pb, AF.Copy,
                      [bankA(it % 4)], [(dkey, "all")])

    def emit_phaseA():
        ar.mark()
        WwT = ar.alloc("WwT", [128, KTD, D], BF16)
        emit_wT(WwT, H["Ww"], "WwT")
        x0t = ar.alloc("x0t", [128, 2048], F32)
        sq = ar.alloc("sqA", [128, 2048], BF16)
        ln1b = ar.alloc("ln1b", [128, KTD, 2048], BF16)
        ln1f = ar.alloc("ln1f", [128, 2048], F32)
        bmu = ar.alloc("bmu", [128, 2048], F32)
        bis = ar.alloc("bis", [128, 2048], F32)
        mu_r = ar.alloc("mu_rA", [1, 2048], F32)
        is_r = ar.alloc("is_rA", [1, 2048], F32)
        tmp_r = ar.alloc("tmp_rA", [1, 2048], F32)
        row4 = ar.alloc("row4", [4, 2048], F32)
        rnb = ar.alloc_at("rnb", [128, 2048], F32, ar.offsets["bmu"])
        vtb = ar.alloc_at("vtb", [128, 1024], BF16, ar.offsets["bis"])

        def ln_rows(mu_ap, is_ap, tmp_ap, psum_sum, psum_sq, rkeys, wk_mu, wk_is):
            V_ts(mu_ap, psum_sum, 1.0 / D, ALU.mult, rkeys, [wk_mu])
            V_ts(is_ap, psum_sq, 1.0 / D, ALU.mult, rkeys, [wk_is],
                 s2=LNEPS, op1=ALU.add)
            V_tt(tmp_ap, mu_ap, mu_ap, ALU.mult, [wk_mu], [("tmp_r",)])
            V_tt(is_ap, is_ap, tmp_ap, ALU.subtract, [wk_is, ("tmp_r",)], [wk_is])
            S_act(is_ap, is_ap, AF.Ln, [wk_is], [wk_is])
            S_act(is_ap, is_ap, AF.Exp, [wk_is], [wk_is], scale=-0.5)

        for cb in range(2):
            csl = slice(cb * 2048, (cb + 1) * 2048)
            for kt in range(KTD):
                kx = ("x0t",)
                DMA("Y", x0t, H["x0"][kt * 128:(kt + 1) * 128, csl], [], [kx], tag="A.x0")
                S_act(sq, x0t, AF.Square, [kx], [("sqA",)])
                for ch in range(4):
                    T_mm(PSA[0:1, ch * 512:(ch + 1) * 512], onesf[:, 0:1],
                         x0t[:, ch * 512:(ch + 1) * 512], kt == 0, kt == KTD - 1,
                         [kx, ("c", "onesf")], [bankA(ch)], tag="A.sum")
                    T_mm(PSB[0:1, ch * 512:(ch + 1) * 512], onesb[:, 0:1],
                         sq[:, ch * 512:(ch + 1) * 512], kt == 0, kt == KTD - 1,
                         [("sqA",), ("c", "ones")], [bankB(ch)], tag="A.sumsq")
            ln_rows(mu_r, is_r, tmp_r, PSA[0:1, 0:2048], PSB[0:1, 0:2048],
                    [bankA(0), bankA(1), bankA(2), bankA(3),
                     bankB(0), bankB(1), bankB(2), bankB(3)],
                    ("mu_rA",), ("is_rA",))
            for ch in range(4):
                T_mm(PSA[:, ch * 512:(ch + 1) * 512], onesf[0:1, :],
                     mu_r[0:1, ch * 512:(ch + 1) * 512],
                     True, True, [("c", "onesf"), ("mu_rA",)], [bankA(ch)], tag="A.bmu")
                T_mm(PSB[:, ch * 512:(ch + 1) * 512], onesf[0:1, :],
                     is_r[0:1, ch * 512:(ch + 1) * 512],
                     True, True, [("c", "onesf"), ("is_rA",)], [bankB(ch)], tag="A.bis")
            S_act(bmu, PSA[:, 0:2048], AF.Copy,
                  [bankA(0), bankA(1), bankA(2), bankA(3)], [("bmu",)])
            S_act(bis, PSB[:, 0:2048], AF.Copy,
                  [bankB(0), bankB(1), bankB(2), bankB(3)], [("bis",)])
            for kt in range(KTD):
                kx = ("x0t",)
                DMA("Y", x0t, H["x0"][kt * 128:(kt + 1) * 128, csl], [], [kx], tag="A.x0b")
                V_tt(ln1f, x0t, bmu, ALU.subtract, [kx, ("bmu",)], [("ln1f",)])
                V_tt(ln1f, ln1f, bis, ALU.mult, [("ln1f",), ("bis",)], [("ln1f",)], eng="P")
                DMA("Y", ln1_hbm[kt * 128:(kt + 1) * 128, csl], ln1f, [("ln1f",)],
                    [("ln1_hbm", kt, cb)], tag="A.ln1out")
                S_act(ln1b[:, kt, :], ln1f, AF.Copy, [("ln1f",)], [("ln1b", kt)])
            sc.link(("rnbrow4",), [("bmu",), ("bis",)])
            for kt in range(KTD):
                S_act(sq, ln1b[:, kt, :], AF.Square, [("ln1b", kt)], [("sqA",)])
                for ch in range(4):
                    T_mm(PSA[0:DN_HEADS, ch * 512:(ch + 1) * 512],
                         C["ind_dn"][:, kt * 4:(kt + 1) * 4],
                         sq[:, ch * 512:(ch + 1) * 512], kt == 0, kt == KTD - 1,
                         [("sqA",), ("c", "ind_dn")], [bankA(ch)], tag="A.rksum")
            V_ts(row4, PSA[0:DN_HEADS, 0:2048], float(np.float32(1e-24)), ALU.max,
                 [bankA(0), bankA(1), bankA(2), bankA(3)], [("row4k",)])
            S_act(row4, row4, AF.Ln, [("row4k",)], [("row4k",)])
            S_act(row4, row4, AF.Exp, [("row4k",)], [("row4k",)], scale=-0.5)
            for h in range(DN_HEADS):
                sel = C["sel_dn"][:, 2 * h * 128:(2 * h + 1) * 128]
                for ch in range(4):
                    T_mm(PSB[:, ch * 512:(ch + 1) * 512], sel,
                         row4[:, ch * 512:(ch + 1) * 512],
                         True, True, [("row4k",), ("c", "sel_dn")], [bankB(ch)],
                         tag="A.rnbc")
                S_act(rnb, PSB[:, 0:2048], AF.Copy,
                      [bankB(0), bankB(1), bankB(2), bankB(3)], [("rnbrow4",)])
                for j in range(2):
                    kt = 2 * h + j
                    V_tt(rk[:, kt, csl], ln1b[:, kt, :], rnb,
                         ALU.mult, [("ln1b", kt), ("rnbrow4",)],
                         [("rk", kt, cb)], eng="P")
            if "wwt" in dbg and cb == 0:
                for kt in range(KTD):
                    DMA("S", dbg["wwt"][kt * 128:(kt + 1) * 128, :], WwT[:, kt, :],
                        [("WwT", "all")], [("dbgwwt", kt)])
            sc.link(("vtb",), [("rnbrow4",)])
            for tt in range(16):
                t0 = tt * 128
                pb = 2 * (tt % 2)
                for k in range(KTD):
                    for fc in range(2):
                        T_mm(PSA[:, (pb + fc) * 512:(pb + fc + 1) * 512],
                             ln1b[:, k, t0:t0 + 128],
                             WwT[:, k, fc * 512:(fc + 1) * 512],
                             k == 0, k == KTD - 1,
                             [("ln1b", k), ("WwT", "all")], [bankA(pb + fc)],
                             tag="A.v")
                S_act(vtb, PSA[:, pb * 512:(pb + 2) * 512], AF.Copy,
                      [bankA(pb), bankA(pb + 1)], [("vtb",)])
                DMA("Y", v_hbm[cb * 2048 + t0:cb * 2048 + t0 + 128, :], vtb,
                    [("vtb",)], [("v_hbm", cb, tt)], tag="A.vout")
        ar.reset()

    if PHASE_LIMIT >= 1:
        emit_phaseA()
        sc.barrier()

    def emit_phaseB():
        ar.mark()
        dn_dbg = {}
        if os.environ.get("MK_DEBUG_DN"):
            for pr in (0, 2):
                dn_dbg[pr] = nc.dram_tensor(f"dn_dbg{pr}", [128, 1280], BF16,
                                            kind="ExternalOutput").ap()
        Sfp = [ar.alloc(f"Sfp{p}", [128, 2, DN_D], F32) for p in range(N_PAIR)]
        Sb = [ar.alloc(f"Sb{p}", [128, 2, DN_D], BF16) for p in range(N_PAIR)]
        for p in range(N_PAIR):
            memset(Sfp[p], 0.0, [], [("Sfp", p)])
            memset(Sb[p], 0.0, [], [("Sb", p)], eng="P")
        NSLOT = 4
        vch = [ar.alloc(f"vch{i}", [64, DN_D], BF16) for i in range(NSLOT)]
        w0t = [ar.alloc(f"w0t{i}", [128, 2, 64], BF16) for i in range(NSLOT)]
        Nb = [ar.alloc(f"Nb{i}", [64, 64], BF16) for i in range(NSLOT)]
        Mb = [ar.alloc(f"Mb{i}", [64, 64], BF16) for i in range(NSLOT)]
        Sacc = [[ar.alloc(f"Sacc{i}_{j}", [64, 64], BF16) for j in range(2)]
                for i in range(NSLOT)]
        Pb = [[ar.alloc(f"Pb{i}_{j}", [64, 128], BF16) for j in range(2)]
              for i in range(NSLOT)]
        ATb = [ar.alloc(f"ATb{i}", [64, 64], BF16) for i in range(NSLOT)]
        intraT = [ar.alloc(f"intraT{i}", [64, 64], BF16) for i in range(NSLOT)]
        wgpt = [ar.alloc(f"wgpt{i}", [64, 256], BF16) for i in range(NSLOT)]
        rkgT = [ar.alloc(f"rkgT{i}", [128, 2, 64], BF16) for i in range(NSLOT)]
        ub = [ar.alloc(f"ub{i}", [64, DN_D], BF16) for i in range(NSLOT)]
        vnb = [ar.alloc(f"vnb{i}", [64, DN_D], BF16) for i in range(NSLOT)]
        otile = [ar.alloc(f"otile{i}", [128, 2, 64], BF16) for i in range(NSLOT)]

        for n in range(DN_CHUNKS):
            for pr in range(N_PAIR):
                s_, h_ = divmod(pr, DN_HEADS)
                i = (n * N_PAIR + pr) % NSLOT
                P = (PSA, PSB)[i % 2]
                nm = ("PSA", "PSB")[i % 2]
                bb = 2 * ((i // 2) % 2)
                q0 = bb * 512
                q1 = (bb + 1) * 512
                k0, k1 = (nm, bb), (nm, bb + 1)
                c0 = s_ * T + n * 64
                kc = ("chunk", i)
                rkread = [("rk", 2 * h_, s_), ("rk", 2 * h_ + 1, s_)]
                if n == 0:
                    memset(w0t[i][:, :, 0:1], 0.0, [], [(kc, "w")])
                    V_copy(w0t[i][:, :, 1:64],
                           rk[:, 2 * h_:2 * h_ + 2, c0:c0 + 63],
                           rkread, [(kc, "w")], eng="P")
                    wsl = [w0t[i][:, j, :] for j in range(2)]
                    wread = [(kc, "w")]
                else:
                    wsl = [rk[:, 2 * h_ + j, c0 - 1:c0 + 63] for j in range(2)]
                    wread = rkread
                rsl = [rk[:, 2 * h_ + j, c0:c0 + 64] for j in range(2)]
                DMA("S", vch[i], v_hbm[c0:c0 + 64, h_ * DN_D:(h_ + 1) * DN_D],
                    [], [(kc, "v")], tag="B.v")
                T_mm(P[0:64, q0:q0 + 64], wsl[0], wsl[0], True, False, wread, [k0], "B.ww")
                T_mm(P[0:64, q0:q0 + 64], wsl[1], wsl[1], False, True, wread, [k0], "B.ww")
                T_mm(P[0:64, q0 + 64:q0 + 128], wsl[0], rsl[0], True, False,
                     wread + rkread, [k0], "B.wr")
                T_mm(P[0:64, q0 + 64:q0 + 128], wsl[1], rsl[1], False, True,
                     wread + rkread, [k0], "B.wr")
                V_tt(Nb[i], P[0:64, q0:q0 + 64], mlow_neg[h_], ALU.mult,
                     [k0, ("mlo", h_)], [(kc, "N")])
                V_tt(Mb[i], P[0:64, q0:q0 + 64], mup_neg[h_], ALU.mult,
                     [k0, ("mun", h_)], [(kc, "M")])
                V_tt(intraT[i], P[0:64, q0 + 64:q0 + 128], mup_pos[h_], ALU.mult,
                     [k0, ("mup", h_)], [(kc, "iT")])
                V_tt(Sacc[i][0], Nb[i], C["eye64b"][0:64, :], ALU.add,
                     [(kc, "N"), ("c", "eye64b")], [(kc, "S0")], eng="P")
                curS = 0
                nKey = mKey = None
                Nr, Mr = Nb[i], Mb[i]
                nKey, mKey = (kc, "N"), (kc, "M")
                for kit in range(1, 6):
                    if kit < 5:
                        T_mm(P[0:64, q1:q1 + 64], Mr, Nr, True, True,
                             [nKey, mKey], [k1], "B.Pn")
                    T_mm(P[0:64, q1 + 64:q1 + 128], Nr, Mr, True, True,
                         [nKey, mKey], [k1], "B.Ptn")
                    j = kit % 2
                    S_act(Pb[i][j], P[0:64, q1:q1 + 128], AF.Copy, [k1],
                          [(kc, "P", j)])
                    T_mm(P[0:64, q1 + 128:q1 + 192], Pb[i][j][:, 64:128], Sacc[i][curS],
                         True, True, [(kc, "P", j), (kc, f"S{curS}")], [k1],
                         "B.Sup")
                    newS = 1 - curS
                    V_tt(Sacc[i][newS], Sacc[i][curS], P[0:64, q1 + 128:q1 + 192],
                         ALU.add, [(kc, f"S{curS}"), k1], [(kc, f"S{newS}")])
                    curS = newS
                    Nr, Mr = Pb[i][j][:, 0:64], Pb[i][j][:, 64:128]
                    nKey = mKey = (kc, "P", j)
                pbT = P[:, q1 + 192:q1 + 224].bitcast(BF16)[0:64, 0:64]
                T_tr(pbT, Sacc[i][curS], eye[0:64, 0:64],
                     [(kc, f"S{curS}"), ("c", "eye")], [k1], "B.AT")
                S_act(ATb[i], pbT, AF.Copy, [k1], [(kc, "AT")])
                for j in range(2):
                    pbW = P[:, q1 + 224 + j * 64:q1 + 288 + j * 64].bitcast(BF16)[0:64, 0:128]
                    T_tr(pbW, wsl[j], eye, wread + [("c", "eye")], [k1], "B.wT")
                    S_act(wgpt[i][:, j * 128:(j + 1) * 128], pbW, AF.Copy,
                          [k1, ("gt", h_)], [(kc, "wg")], scale=gtail_col[h_])
                V_tt(rkgT[i], rk[:, 2 * h_:2 * h_ + 2, c0:c0 + 64],
                     gtile[h_].rearrange("p (a c) -> p a c", a=1).broadcast_to([128, 2, 64]),
                     ALU.mult, rkread + [("gtile", h_)], [(kc, "rg")], eng="P")
                T_mm(P[0:64, q0 + 256:q0 + 512], wsl[0], Sb[pr][:, 0, :], True, False,
                     wread + [("Sb", pr)], [k0], "B.wS")
                T_mm(P[0:64, q0 + 256:q0 + 512], wsl[1], Sb[pr][:, 1, :], False, True,
                     wread + [("Sb", pr)], [k0], "B.wS")
                V_stt(ub[i], P[0:64, q0 + 256:q0 + 512], -1.0, vch[i], ALU.mult,
                      ALU.add, [k0, (kc, "v")], [(kc, "u")])
                T_mm(P[0:64, q0:q0 + 256], ATb[i], ub[i], True, True,
                     [(kc, "AT"), (kc, "u")], [k0], "B.Au")
                S_act(vnb[i], P[0:64, q0:q0 + 256], AF.Copy, [k0], [(kc, "vn")])
                for mv in range(2):
                    oc = slice(q1 + 352 + mv * 64, q1 + 352 + (mv + 1) * 64)
                    for kt in range(2):
                        T_mm(P[:, oc], Sb[pr][:, kt, mv * 128:(mv + 1) * 128],
                             rkgT[i][:, kt, :], kt == 0, False,
                             [("Sb", pr), (kc, "rg")], [k1], "B.o1")
                    T_mm(P[:, oc], vnb[i][:, mv * 128:(mv + 1) * 128], intraT[i],
                         False, True, [(kc, "vn"), (kc, "iT")], [k1], "B.o2")
                S_act(otile[i], P[:, q1 + 352:q1 + 480].rearrange("p (a c) -> p a c", a=2),
                      AF.Copy, [k1, ("al", h_)], [(kc, "ot")],
                      scale=al_col[h_])
                DMA("S", o_hbm[2 * h_ * 128:(2 * h_ + 2) * 128, c0:c0 + 64]
                    .rearrange("(a p) c -> p a c", p=128), otile[i],
                    [(kc, "ot")], [("o_hbm", pr, n)], tag="B.oout")
                for kt in range(2):
                    T_mm(P[:, q0 + kt * 256:q0 + (kt + 1) * 256],
                         wgpt[i][:, kt * 128:(kt + 1) * 128], vnb[i],
                         True, True, [(kc, "wg"), (kc, "vn")], [k0], "B.Sup2")
                V_stt(Sfp[pr].rearrange("p a c -> p (a c)"),
                      Sfp[pr].rearrange("p a c -> p (a c)"),
                      gC_col[h_], P[:, q0:q0 + 512], ALU.mult, ALU.add,
                      [("Sfp", pr), k0, ("gC", h_)], [("Sfp", pr)])
                S_act(Sb[pr], Sfp[pr], AF.Copy, [("Sfp", pr)], [("Sb", pr)])
        ar.reset()

    if PHASE_LIMIT >= 2:
        emit_phaseB()
        sc.barrier()

    def emit_phaseC():
        ar.mark()
        WrT = ar.alloc("WrT", [128, KTD, D], BF16)
        emit_wT(WrT, H["Wr"], "WrT")
        ocb = ar.alloc("ocb", [128, KTD, 2048], BF16)
        ln1t = ar.alloc("ln1tC", [128, 2048], F32)
        at = ar.alloc("atC", [128, 2048], F32)
        for cb in range(2):
            csl = slice(cb * 2048, (cb + 1) * 2048)
            for k in range(KTD):
                DMA("Y", ocb[:, k, :], o_hbm[k * 128:(k + 1) * 128, csl],
                    [], [("ocb", k)], tag="C.o")
            for m in range(KTD):
                DMA("Y", ln1t, ln1_hbm[m * 128:(m + 1) * 128, csl],
                    [("ln1_hbm", m, cb)], [("ln1tC",)], tag="C.ln1")
                for k in range(KTD):
                    for ch in range(4):
                        T_mm(PSA[:, ch * 512:(ch + 1) * 512],
                             WrT[:, k, m * 128:(m + 1) * 128],
                             ocb[:, k, ch * 512:(ch + 1) * 512],
                             k == 0, k == KTD - 1,
                             [("WrT", "all"), ("ocb", k)], [bankA(ch)], tag="C.proj")
                V_stt(at, PSA[:, 0:2048], 1.0, ln1t, ALU.mult, ALU.add,
                      [bankA(0), bankA(1), bankA(2), bankA(3), ("ln1tC",)],
                      [("atC",)])
                DMA("Y", a_hbm[m * 128:(m + 1) * 128, csl], at, [("atC",)],
                    [("a_hbm", m, cb)], tag="C.aout")
        ar.reset()

    if PHASE_LIMIT >= 3:
        emit_phaseC()
        sc.barrier()
    if "rk" in dbg:
        for kt in range(KTD):
            DMA("S", dbg["rk"][kt * 128:(kt + 1) * 128, :], rk[:, kt, :],
                [], [("dbgork", kt)])
    if "o" in dbg:
        for kt in range(KTD):
            DMA("S", dbg["o"][kt * 128:(kt + 1) * 128, :],
                o_hbm[kt * 128:(kt + 1) * 128, :], [], [("dbgoo", kt)])
    if "rk" in dbg or "o" in dbg:
        sc.barrier()
    ar.reset()

    def emit_phaseD():
        ar.mark()
        x0t = ar.alloc("x0tD", [128, 2048], F32)
        at = ar.alloc("atD", [128, 2048], F32)
        x1t = ar.alloc("x1tD", [128, 2048], F32)
        xb = ar.alloc("xbD", [128, 2048], BF16)
        sq = ar.alloc("sqD", [128, 2048], BF16)
        bmu = ar.alloc("bmuD", [128, 2048], F32)
        bis = ar.alloc("bisD", [128, 2048], F32)
        mu_r = ar.alloc("mu_rD", [1, 2048], F32)
        is_r = ar.alloc("is_rD", [1, 2048], F32)
        mu2_r = ar.alloc("mu2_rD", [1, 2048], F32)
        is2_r = ar.alloc("is2_rD", [1, 2048], F32)
        tmp_r = ar.alloc("tmp_rD", [1, 2048], F32)

        def rowstats(mu_ap, is_ap, wk_mu, wk_is, keys):
            V_ts(mu_ap, PSA[0:1, 0:2048], 1.0 / D, ALU.mult, keys, [wk_mu])
            V_ts(is_ap, PSA[32:33, 0:2048], 1.0 / D, ALU.mult, keys, [wk_is],
                 s2=LNEPS, op1=ALU.add)
            V_tt(tmp_r, mu_ap, mu_ap, ALU.mult, [wk_mu], [("tmp_rD",)])
            V_tt(is_ap, is_ap, tmp_r, ALU.subtract, [wk_is, ("tmp_rD",)], [wk_is])
            S_act(is_ap, is_ap, AF.Ln, [wk_is], [wk_is])
            S_act(is_ap, is_ap, AF.Exp, [wk_is], [wk_is], scale=-0.5)

        def statpass(src_hbm, src_key, wk_mu, wk_is, mu_ap, is_ap, cb, tag):
            csl = slice(cb * 2048, (cb + 1) * 2048)
            for m in range(KTD):
                DMA("Y", x1t, src_hbm[m * 128:(m + 1) * 128, csl],
                    [(src_key, m, cb)], [("x1tD",)], tag=tag)
                S_act(xb, x1t, AF.Copy, [("x1tD",)], [("xbD",)])
                S_act(sq, xb, AF.Square, [("xbD",)], [("sqD",)])
                for ch in range(4):
                    T_mm(PSA[0:1, ch * 512:(ch + 1) * 512], onesb[:, 0:1],
                         xb[:, ch * 512:(ch + 1) * 512], m == 0, False,
                         [("xbD",), ("c", "ones")], [bankA(ch)], tag=tag + "s",
                         sgc=True)
                    T_mm(PSA[32:33, ch * 512:(ch + 1) * 512], onesb[:, 0:1],
                         sq[:, ch * 512:(ch + 1) * 512], False, m == KTD - 1,
                         [("sqD",), ("c", "ones")], [bankA(ch)], tag=tag + "q",
                         sgc=True)
            rowstats(mu_ap, is_ap, wk_mu, wk_is,
                     [bankA(0), bankA(1), bankA(2), bankA(3)])

        def bcast(mu_ap, is_ap, rk_mu, rk_is):
            for ch in range(4):
                T_mm(PSA[:, ch * 512:(ch + 1) * 512], onesf[0:1, :],
                     mu_ap[0:1, ch * 512:(ch + 1) * 512], True, True,
                     [("c", "onesf"), rk_mu], [bankA(ch)], tag="D.bmu")
                T_mm(PSB[:, ch * 512:(ch + 1) * 512], onesf[0:1, :],
                     is_ap[0:1, ch * 512:(ch + 1) * 512], True, True,
                     [("c", "onesf"), rk_is], [bankB(ch)], tag="D.bis")
            S_act(bmu, PSA[:, 0:2048], AF.Copy,
                  [bankA(0), bankA(1), bankA(2), bankA(3)], [("bmuD",)])
            S_act(bis, PSB[:, 0:2048], AF.Copy,
                  [bankB(0), bankB(1), bankB(2), bankB(3)], [("bisD",)])

        for cb in range(2):
            csl = slice(cb * 2048, (cb + 1) * 2048)
            statpass(a_hbm, "a_hbm", ("muD",), ("isD",), mu_r, is_r, cb, "D.a1")
            bcast(mu_r, is_r, ("muD",), ("isD",))
            for m in range(KTD):
                DMA("Y", x0t, H["x0"][m * 128:(m + 1) * 128, csl], [],
                    [("x0tD",)], tag="D.x0")
                DMA("Y", at, a_hbm[m * 128:(m + 1) * 128, csl],
                    [("a_hbm", m, cb)], [("atD",)], tag="D.a2")
                V_tt(at, at, bmu, ALU.subtract, [("atD",), ("bmuD",)], [("atD",)])
                V_tt(at, at, bis, ALU.mult, [("atD",), ("bisD",)], [("atD",)], eng="P")
                V_tt(x1t, x0t, at, ALU.add, [("x0tD",), ("atD",)], [("x1tD",)])
                DMA("Y", x1_hbm[m * 128:(m + 1) * 128, csl], x1t, [("x1tD",)],
                    [("x1_hbm", m, cb)], tag="D.x1out")
                S_act(xb, x1t, AF.Copy, [("x1tD",)], [("xbD",)])
                S_act(sq, xb, AF.Square, [("xbD",)], [("sqD",)])
                for ch in range(4):
                    T_mm(PSA[0:1, ch * 512:(ch + 1) * 512], onesb[:, 0:1],
                         xb[:, ch * 512:(ch + 1) * 512], m == 0, False,
                         [("xbD",), ("c", "ones")], [bankA(ch)], tag="D.s2",
                         sgc=True)
                    T_mm(PSA[32:33, ch * 512:(ch + 1) * 512], onesb[:, 0:1],
                         sq[:, ch * 512:(ch + 1) * 512], False, m == KTD - 1,
                         [("sqD",), ("c", "ones")], [bankA(ch)], tag="D.q2",
                         sgc=True)
            rowstats(mu2_r, is2_r, ("mu2D",), ("is2D",),
                     [bankA(0), bankA(1), bankA(2), bankA(3)])
            bcast(mu2_r, is2_r, ("mu2D",), ("is2D",))
            for m in range(KTD):
                DMA("Y", x1t, x1_hbm[m * 128:(m + 1) * 128, csl],
                    [("x1_hbm", m, cb)], [("x1tD",)], tag="D.x1in")
                V_tt(at, x1t, bmu, ALU.subtract, [("x1tD",), ("bmuD",)], [("atD",)])
                V_tt(at, at, bis, ALU.mult, [("atD",), ("bisD",)], [("atD",)], eng="P")
                DMA("Y", xin_hbm[m * 128:(m + 1) * 128, csl], at, [("atD",)],
                    [("xin_hbm", m, cb)], tag="D.xinout")
        ar.reset()

    if PHASE_LIMIT >= 4:
        emit_phaseD()
        sc.barrier()

    emit_inner = build_inner(nc, sc, es, ar, H, CK, C, dbg, locals())

    for l in range(IN_DEPTH):
        if PHASE_LIMIT >= 5 + l:
            emit_inner(l)
            if l == IN_DEPTH - 1:
                sc.barrier()

    def emit_phaseG():
        ar.mark()
        ft = ar.alloc("ftG", [128, 2048], F32)
        x1t = ar.alloc("x1tG", [128, 2048], F32)
        xb = ar.alloc("xbG", [128, 2048], BF16)
        sq = ar.alloc("sqG", [128, 2048], BF16)
        bmu = ar.alloc("bmuG", [128, 2048], F32)
        bis = ar.alloc("bisG", [128, 2048], F32)
        mu_r = ar.alloc("mu_rG", [1, 2048], F32)
        is_r = ar.alloc("is_rG", [1, 2048], F32)
        tmp_r = ar.alloc("tmp_rG", [1, 2048], F32)
        for cb in range(2):
            csl = slice(cb * 2048, (cb + 1) * 2048)
            for m in range(KTD):
                DMA("Y", ft, xin_hbm[m * 128:(m + 1) * 128, csl],
                    [("xin_hbm", m, cb)], [("ftG",)], tag="G.f")
                S_act(xb, ft, AF.Copy, [("ftG",)], [("xbG",)])
                S_act(sq, xb, AF.Square, [("xbG",)], [("sqG",)])
                for ch in range(4):
                    T_mm(PSA[0:1, ch * 512:(ch + 1) * 512], onesb[:, 0:1],
                         xb[:, ch * 512:(ch + 1) * 512], m == 0, False,
                         [("xbG",), ("c", "ones")], [bankA(ch)], tag="G.sum",
                         sgc=True)
                    T_mm(PSA[32:33, ch * 512:(ch + 1) * 512], onesb[:, 0:1],
                         sq[:, ch * 512:(ch + 1) * 512], False, m == KTD - 1,
                         [("sqG",), ("c", "ones")], [bankA(ch)], tag="G.sumsq",
                         sgc=True)
            V_ts(mu_r, PSA[0:1, 0:2048], 1.0 / D, ALU.mult,
                 [bankA(0), bankA(1), bankA(2), bankA(3)], [("muG",)])
            V_ts(is_r, PSA[32:33, 0:2048], 1.0 / D, ALU.mult,
                 [bankA(0), bankA(1), bankA(2), bankA(3)], [("isG",)],
                 s2=LNEPS, op1=ALU.add)
            V_tt(tmp_r, mu_r, mu_r, ALU.mult, [("muG",)], [("tmpG",)])
            V_tt(is_r, is_r, tmp_r, ALU.subtract, [("isG",), ("tmpG",)], [("isG",)])
            S_act(is_r, is_r, AF.Ln, [("isG",)], [("isG",)])
            S_act(is_r, is_r, AF.Exp, [("isG",)], [("isG",)], scale=-0.5)
            for ch in range(4):
                T_mm(PSA[:, ch * 512:(ch + 1) * 512], onesf[0:1, :],
                     mu_r[0:1, ch * 512:(ch + 1) * 512], True, True,
                     [("c", "onesf"), ("muG",)], [bankA(ch)], tag="G.bmu")
                T_mm(PSB[:, ch * 512:(ch + 1) * 512], onesf[0:1, :],
                     is_r[0:1, ch * 512:(ch + 1) * 512], True, True,
                     [("c", "onesf"), ("isG",)], [bankB(ch)], tag="G.bis")
            S_act(bmu, PSA[:, 0:2048], AF.Copy,
                  [bankA(0), bankA(1), bankA(2), bankA(3)], [("bmuG",)])
            S_act(bis, PSB[:, 0:2048], AF.Copy,
                  [bankB(0), bankB(1), bankB(2), bankB(3)], [("bisG",)])
            for m in range(KTD):
                DMA("Y", ft, xin_hbm[m * 128:(m + 1) * 128, csl],
                    [("xin_hbm", m, cb)], [("ftG",)], tag="G.f2")
                DMA("Y", x1t, x1_hbm[m * 128:(m + 1) * 128, csl],
                    [("x1_hbm", m, cb)], [("x1tG",)], tag="G.x1")
                V_tt(ft, ft, bmu, ALU.subtract, [("ftG",), ("bmuG",)], [("ftG",)])
                V_tt(ft, ft, bis, ALU.mult, [("ftG",), ("bisG",)], [("ftG",)], eng="P")
                V_tt(ft, ft, x1t, ALU.add, [("ftG",), ("x1tG",)], [("ftG",)])
                DMA("Y", out_h[m * 128:(m + 1) * 128, csl], ft, [("ftG",)],
                    [("out", m, cb)], tag="G.out")
        ar.reset()

    if PHASE_LIMIT >= 7:
        emit_phaseG()
    else:
        sc.barrier()
        zt = ar.alloc("zout", [128, 2048], F32)
        memset(zt, 0.0, [], [("zout",)])
        for m in range(KTD):
            for cb in range(2):
                DMA("Y", out_h[m * 128:(m + 1) * 128, cb * 2048:(cb + 1) * 2048],
                    zt, [("zout",)], [("out", m, cb)], tag="zero.out")

    def dump(name, src_hbm):
        if name in dbg:
            tt = ar.alloc("dbgt" + name, [128, 2048], dbg[name].tensor.dtype)
            rows = dbg[name].shape[0]
            for m in range(rows // 128):
                for cb in range(2):
                    DMA("S", tt, src_hbm[m * 128:(m + 1) * 128,
                                         cb * 2048:(cb + 1) * 2048],
                        [], [("dbgt", name)], tag="dbg")
                    DMA("S", dbg[name][m * 128:(m + 1) * 128,
                                       cb * 2048:(cb + 1) * 2048],
                        tt, [("dbgt", name)], [("dbgo", name, m, cb)], tag="dbg")

    if dbg:
        sc.barrier()
        dump("ln1", ln1_hbm)
        dump("a", a_hbm)
        dump("x1", x1_hbm)
        dump("xin", xin_hbm)
        if "v" in dbg:
            tt = ar.alloc("dbgtv", [128, 1024], BF16)
            for m in range(M // 128):
                DMA("S", tt, v_hbm[m * 128:(m + 1) * 128, :], [], [("dbgtv",)])
                DMA("S", dbg["v"][m * 128:(m + 1) * 128, :], tt, [("dbgtv",)],
                    [("dbgov", m)])

    sc.materialize(nc, es)
    return nc, es


def build_inner(nc, sc, es, ar, H, CK, C, dbg, O):
    T_mm, T_tr, S_act = O["T_mm"], O["T_tr"], O["S_act"]
    V_tt, V_stt, V_ts = O["V_tt"], O["V_stt"], O["V_ts"]
    DMA, V_copy, V_red, memset = O["DMA"], O["V_copy"], O["V_red"], O["memset"]
    PSA, PSB = O["PSA"], O["PSB"]
    bankA, bankB = O["bankA"], O["bankB"]
    eye, onesb, onesf = O["eye"], O["onesb"], O["onesf"]
    xin_hbm = O["xin_hbm"]
    CK = O["CK"]
    wqkv_bf, wo_bf, w1_bf, w2_bf = O["wqkv_bf"], O["wo_bf"], O["w1_bf"], O["w2_bf"]


    PREVKEYS = [[], []]

    def call_fence(mykeys):
        allk = PREVKEYS[0] + PREVKEYS[1]
        for k in mykeys:
            sc.link(k, allk)
        PREVKEYS[1] = PREVKEYS[0]
        PREVKEYS[0] = list(mykeys)

    def emit_inner(l):
        ar.mark()
        lk = ("L", l)
        if l > 0:
            ok = ("L", l - 1)
            oldc = ([(ok, nm, j) for nm in ("qs", "ks") for j in range(3)]
                    + [(ok, "lg"), (ok, "lb"), (ok, "b2"), (ok, "b1")])
            for nm in ("qs", "ks"):
                for j in range(3):
                    sc.link((lk, nm, j), oldc)
            for nm in ("lg", "lb", "b2", "b1"):
                sc.link((lk, nm), oldc)
        qs_c = ar.alloc("qs_c", [128, 2], F32)
        ks_c = ar.alloc("ks_c", [128, 2], F32)
        for (nm, hsrc, dst) in (("qs", H["q_scale"], qs_c), ("ks", H["k_scale"], ks_c)):
            DMA("Y", dst[0:64, 0:1], hsrc[l:l + 1, :], [], [(lk, nm, 0)], tag="I.qs")
            V_copy(dst[0:32, 1:2], dst[32:64, 0:1], [(lk, nm, 0)], [(lk, nm, 1)])
            V_copy(dst[32:64, 1:2], dst[0:32, 0:1], [(lk, nm, 0)], [(lk, nm, 1)])
            V_copy(dst[64:128, 0:2], dst[0:64, 0:2],
                   [(lk, nm, 0), (lk, nm, 1)], [(lk, nm, 2)])
        lg_c = ar.alloc("lg_c", [128, KTD], F32)
        lb_c = ar.alloc("lb_c", [128, KTD], F32)
        b2_c = ar.alloc("b2_c", [128, KTD], F32)
        for kt in range(KTD):
            DMA("Y", lg_c[:, kt:kt + 1], H["ln_g"][l:l + 1, kt * 128:(kt + 1) * 128],
                [], [(lk, "lg")], tag="I.lg")
            DMA("Y", lb_c[:, kt:kt + 1], H["ln_b"][l:l + 1, kt * 128:(kt + 1) * 128],
                [], [(lk, "lb")], tag="I.lb")
            DMA("Y", b2_c[:, kt:kt + 1], H["b2"][l:l + 1, kt * 128:(kt + 1) * 128],
                [], [(lk, "b2")], tag="I.b2")
        b1a_c = ar.alloc("b1a_c", [128, 22], F32)
        b1g_c = ar.alloc("b1g_c", [128, 22], F32)
        for mi in range(22):
            n = min(128, HID - mi * 128)
            DMA("Y", b1a_c[0:n, mi:mi + 1], H["b1"][l:l + 1, mi * 128:mi * 128 + n],
                [], [(lk, "b1")], tag="I.b1")
            DMA("Y", b1g_c[0:n, mi:mi + 1],
                H["b1"][l:l + 1, HID + mi * 128:HID + mi * 128 + n],
                [], [(lk, "b1")], tag="I.b1")
        for s in range(SEQ):
            emit_attention(l, s, lk, qs_c, ks_c)
            if l == 0 and "xwo" in dbg:
                for kt in range(KTD):
                    DMA("S", dbg["xwo"][kt * 128:(kt + 1) * 128, s * T:(s + 1) * T],
                        xin_hbm[kt * 128:(kt + 1) * 128, s * T:(s + 1) * T],
                        [("xin_hbm", kt, s)], [("dbgxwo", kt, s)])
            emit_chain(l, s, lk, lg_c, lb_c, b1a_c, b1g_c, b2_c)
        ar.reset()

    def emit_attention(l, s, lk, qs_c, ks_c):
        ar.mark()
        sk = ("A", l, s)
        col0 = s * T
        call_fence(
            [(sk, "hb", kt) for kt in range(KTD)]
            + [(sk, "oT"), (sk, "vT"), (sk, "qx"), (sk, "kx"), ("xtI",),
               (sk, "scr"), (sk, "swb"), (sk, "nrm", "q"), (sk, "nrm", "k"),
               (sk, "kmd"), (sk, "Rq")]
            + [(sk, "E", r) for r in range(8)]
            + [(sk, nm, k) for nm in ("qraw", "kraw", "vbuf") for k in range(4)]
            + [(sk, nm, p) for nm in ("ot1", "ot2", "rr") for p in range(2)]
            + [(sk, "TB", nm) for nm in ("cq", "sq", "ck", "sk")]
            + [(sk, nm, g) for nm in ("wq", "wot") for g in range(2)])
        hb = ar.alloc("hb", [128, KTD, T], BF16)
        oT = ar.alloc_at("oT", [128, 4, T], BF16, ar.offsets["hb"])
        Esb = ar.alloc_at("Esb", [128, 8, 512], BF16, ar.offsets["hb"] + 16384)
        qext = ar.alloc_at("qext", [65, T], BF16, ar.offsets["hb"] + 16384 + 8192)
        kext = ar.alloc_at("kext", [65, T], BF16, ar.offsets["hb"] + 16384 + 8192 + 4096)
        vT = ar.alloc("vTI", [128, 16, 130], BF16)
        qraw = ar.alloc("qraw", [128, 4, T], BF16)
        kraw = ar.alloc("kraw", [128, 4, T], BF16)
        vbuf = ar.alloc("vbuf", [128, 4, T], BF16)
        xt = ar.alloc("xtI", [128, T], F32)
        scr = ar.alloc("scrI", [128, T], BF16)
        swb = ar.alloc("swbI", [128, T], BF16)
        nrm_q = ar.alloc("nrmq", [8, T], F32)
        nrm_k = ar.alloc("nrmk", [8, T], F32)
        kmdw = ar.alloc("kmdw", [128, 4, 4], F32)
        Rq = ar.alloc("Rq", [8, T], BF16)
        otmp = [ar.alloc(f"otmpI{i}", [64, 512], BF16) for i in range(2)]
        otmp2 = [ar.alloc(f"otmpI2{i}", [64, 512], BF16) for i in range(2)]
        rrow = [ar.alloc(f"rrowI{i}", [128, 512], F32) for i in range(2)]
        TB = {}
        for nm in ("cq", "sq", "ck", "sk"):
            TB[nm] = ar.alloc("TB" + nm, [128, T], BF16)
        wqkv_t = [ar.alloc(f"wqkv_t{i}", [128, KTD, 128], BF16) for i in range(2)]
        wo_t = [ar.alloc(f"wo_t{i}", [128, 4, 128], BF16) for i in range(2)]
        for nm, base, scol, sidx in (("cq", "cs_q", qs_c, 0), ("sq", "ss_q", qs_c, 1),
                                     ("ck", "cs_k", ks_c, 0), ("sk", "ss_k", ks_c, 1)):
            DMA("Y", xt, CK[base], [], [("xtI",)], tag="I.tb")
            S_act(TB[nm], xt, AF.Copy,
                  [("xtI",), (lk, "qs", 2), (lk, "ks", 2)], [(sk, "TB", nm)],
                  scale=scol[:, sidx:sidx + 1])
        for kt in range(KTD):
            kx = ("xtI",)
            DMA("Y", xt, xin_hbm[kt * 128:(kt + 1) * 128, col0:col0 + T],
                [("xin_hbm", kt, s)], [kx], tag="I.h.x")
            if kt < 4:
                S_act(hb[:, kt, :], xt, AF.Copy, [kx], [(sk, "hb", kt)])
            else:
                memset(hb[:, kt, 0:1], 0.0, [], [(sk, "hb", kt)])
                S_act(hb[:, kt, 1:T], xt[:, 0:T - 1], AF.Copy,
                      [kx], [(sk, "hb", kt)])
        for mt in range(12):
            gi = mt % 2
            tb = (l * 12 + mt) * 128
            DMA("S", wqkv_t[gi][:],
                wqkv_bf[tb:tb + 128, :].rearrange("p (a c) -> p a c", a=KTD),
                [("wqi", l, mt)], [(sk, "wq", gi)], tag="I.wqkv")
            P, nm = (PSA, "PSA") if mt % 2 == 0 else (PSB, "PSB")
            for kt in range(KTD):
                for ch in range(4):
                    T_mm(P[:, ch * 512:(ch + 1) * 512],
                         wqkv_t[gi][:, kt, :],
                         hb[:, kt, ch * 512:(ch + 1) * 512],
                         kt == 0, kt == KTD - 1,
                         [(sk, "wq", gi), (sk, "hb", kt)], [(nm, ch)], tag="I.qkv")
            if mt < 4:
                dst, dk = qraw[:, mt, :], (sk, "qraw", mt)
            elif mt < 8:
                dst, dk = kraw[:, mt - 4, :], (sk, "kraw", mt - 4)
            else:
                dst, dk = vbuf[:, mt - 8, :], (sk, "vbuf", mt - 8)
            S_act(dst, P[:, 0:T], AF.Copy,
                  [(nm, 0), (nm, 1), (nm, 2), (nm, 3)], [dk], tag="I.qkv.ev")
        for gname, raw, tcos, tsin in (("q", qraw, TB["cq"], TB["sq"]),
                                       ("k", kraw, TB["ck"], TB["sk"])):
            rkey = "qraw" if gname == "q" else "kraw"
            for kt in range(4):
                S_act(scr, raw[:, kt, :], AF.Square, [(sk, rkey, kt)], [(sk, "scr")])
                for ch in range(4):
                    T_mm(PSA[0:8, ch * 512:(ch + 1) * 512],
                         C["ind_in"][:, kt * 8:(kt + 1) * 8],
                         scr[:, ch * 512:(ch + 1) * 512], kt == 0, kt == 3,
                         [(sk, "scr"), ("c", "ind_in")], [bankA(ch)], tag="I.ss")
            nrm = nrm_q if gname == "q" else nrm_k
            nkey = (sk, "nrm", gname)
            V_ts(nrm, PSA[0:8, 0:T], float(np.float32(1e-24)), ALU.max,
                 [bankA(0), bankA(1), bankA(2), bankA(3)], [nkey])
            S_act(nrm, nrm, AF.Ln, [nkey], [nkey])
            S_act(nrm, nrm, AF.Exp, [nkey], [nkey], scale=-0.5)
            for kt in range(4):
                sel = C["sel_in"][:, kt * 128:(kt + 1) * 128]
                for ch in range(4):
                    T_mm(PSB[:, ch * 512:(ch + 1) * 512], sel,
                         nrm[:, ch * 512:(ch + 1) * 512], True, True,
                         [nkey, ("c", "sel_in")], [bankB(ch)], tag="I.rn")
                V_tt(raw[:, kt, :], raw[:, kt, :], PSB[:, 0:T], ALU.mult,
                     [(sk, rkey, kt), bankB(0), bankB(1), bankB(2), bankB(3)],
                     [(sk, rkey, kt)], tag="I.nrm")
                for ch in range(4):
                    T_mm(PSA[:, ch * 512:(ch + 1) * 512], C["Pswap"],
                         raw[:, kt, ch * 512:(ch + 1) * 512], True, True,
                         [(sk, rkey, kt), ("c", "Pswap")], [bankA(ch)], tag="I.sw")
                V_tt(scr, raw[:, kt, :], tcos, ALU.mult,
                     [(sk, rkey, kt), (sk, "TB", "c" + gname)], [(sk, "scr")],
                     eng="P")
                S_act(swb, PSA[:, 0:T], AF.Copy,
                      [bankA(0), bankA(1), bankA(2), bankA(3)], [(sk, "swb")])
                V_tt(swb, swb, tsin, ALU.mult,
                     [(sk, "swb"), (sk, "TB", "s" + gname)], [(sk, "swb")], eng="P")
                V_tt(raw[:, kt, :], scr, swb, ALU.add,
                     [(sk, "scr"), (sk, "swb")], [(sk, rkey, kt)], tag="I.rot")
        if l == 0 and s == 0 and "qf" in dbg:
            for kt in range(4):
                DMA("S", dbg["qf"][kt * 128:(kt + 1) * 128, 0:T], qraw[:, kt, :],
                    [(sk, "qraw", kt)], [("dbgqf", kt)])
                DMA("S", dbg["kf"][kt * 128:(kt + 1) * 128, 0:T], kraw[:, kt, :],
                    [(sk, "kraw", kt)], [("dbgkf", kt)])
                DMA("S", dbg["vv"][kt * 128:(kt + 1) * 128, 0:T], vbuf[:, kt, :],
                    [(sk, "vbuf", kt)], [("dbgvv", kt)])
        for kt in range(4):
            for w in range(NW):
                V_red(kmdw[:, kt, w:w + 1],
                      kraw[:, kt, w * WINDOW:(w + 1) * WINDOW], ALU.max,
                      [(sk, "kraw", kt)], [(sk, "kmd")], absval=True, tag="I.kmd")
        for kt in range(4):
            for w in range(NW - 1, 0, -1):
                V_tt(kmdw[:, kt, w:w + 1], kmdw[:, kt, w:w + 1],
                     kmdw[:, kt, w - 1:w], ALU.max, [(sk, "kmd")], [(sk, "kmd")],
                     tag="I.kmd2")
        for kt in range(4):
            V_stt(scr, qraw[:, kt, :], -1.0, qraw[:, kt, :], ALU.mult, ALU.max,
                  [(sk, "qraw", kt)], [(sk, "scr")], tag="I.absq")
            for w in range(NW):
                V_ts(swb[:, w * WINDOW:(w + 1) * WINDOW],
                     scr[:, w * WINDOW:(w + 1) * WINDOW], kmdw[:, kt, w:w + 1],
                     ALU.mult, [(sk, "scr"), (sk, "kmd")], [(sk, "swb")],
                     tag="I.bterm")
            for ch in range(4):
                T_mm(PSA[0:8, ch * 512:(ch + 1) * 512],
                     C["ind_in"][:, kt * 8:(kt + 1) * 8],
                     swb[:, ch * 512:(ch + 1) * 512], kt == 0, kt == 3,
                     [(sk, "swb"), ("c", "ind_in")], [bankA(ch)], tag="I.bsum")
        V_ts(Rq, PSA[0:8, 0:T], -1.0, ALU.mult,
             [bankA(0), bankA(1), bankA(2), bankA(3)], [(sk, "Rq")], tag="I.Rq")
        sc.link((sk, "oT"), [(sk, "hb", kt) for kt in range(KTD)])
        for r in range(8):
            sc.link((sk, "E", r), [(sk, "hb", kt) for kt in range(KTD)])
        sc.link((sk, "qx"), [(sk, "hb", kt) for kt in range(KTD)])
        sc.link((sk, "kx"), [(sk, "hb", kt) for kt in range(KTD)])
        EPS30 = float(np.float32(1e-30))
        for h in range(IN_HEADS):
            kt4, r0 = h // 2, (h % 2) * 64
            V_copy(qext[0:64, :], qraw[r0:r0 + 64, kt4, :],
                   [(sk, "qraw", kt4)], [(sk, "qx")], tag="I.qx")
            DMA("S", qext[64:65, :], Rq[h:h + 1, :], [(sk, "Rq")], [(sk, "qx")],
                tag="I.qxB")
            V_copy(kext[0:64, :], kraw[r0:r0 + 64, kt4, :],
                   [(sk, "kraw", kt4)], [(sk, "kx")], tag="I.kx")
            memset(kext[64:65, :], 1.0, [], [(sk, "kx")])
            if h % 2 == 0:
                for tt in range(16):
                    pbt = PSA[:, 1024 + (tt % 2) * 64:1024 + (tt % 2) * 64 + 64]\
                        .bitcast(BF16)[:, 0:128]
                    T_tr(pbt, vbuf[:, kt4, tt * 128:(tt + 1) * 128], eye,
                         [(sk, "vbuf", kt4), ("c", "eye")], [bankA(2)],
                         tag="I.vT")
                    S_act(vT[:, tt, 0:64], pbt[:, 0:64], AF.Copy, [bankA(2)],
                          [(sk, "vT")], tag="I.vTev")
                    S_act(vT[:, tt, 65:129], pbt[:, 64:128], AF.Copy, [bankA(2)],
                          [(sk, "vT")], tag="I.vTev2")
                memset(vT[:, :, 64:65], 1.0, [], [(sk, "vT")])
                memset(vT[:, :, 129:130], 1.0, [], [(sk, "vT")], eng="P")
            vsl = slice(r0 + (h % 2), r0 + (h % 2) + 65)
            for w in range(NW):
                if w == 0:
                    stripes = {r: (r, r, 3, ("maskB", r)) for r in range(4)}
                    avorder = [0, 1, 2, 3]
                else:
                    stripes = {}
                    for r in range(8):
                        mk = ("maskA", r) if r <= 3 else ("maskB", r - 4)
                        stripes[r] = (4 * (w - 1) + r, max(0, r - 4),
                                      min(3, r), mk)
                    avorder = [3, 4, 0, 1, 2, 5, 6, 7]
                for r, (kt_abs, lo, hi, (mknm, mkqt)) in stripes.items():
                    width = (hi - lo + 1) * 128
                    slot = r % 4
                    P2, nm2 = (PSA, "PSA") if slot < 2 else (PSB, "PSB")
                    bk = slot % 2
                    ps = P2[:, bk * 512:bk * 512 + width]
                    T_mm(ps, kext[:, kt_abs * 128:(kt_abs + 1) * 128],
                         qext[:, (4 * w + lo) * 128:(4 * w + hi + 1) * 128],
                         True, True, [(sk, "qx"), (sk, "kx")], [(nm2, bk)],
                         tag="I.sim")
                    mo = (mkqt - lo) * 128
                    V_tt(P2[:, bk * 512 + mo:bk * 512 + mo + 128],
                         P2[:, bk * 512 + mo:bk * 512 + mo + 128],
                         C[mknm], ALU.add,
                         [(nm2, bk), ("c", mknm)], [(nm2, bk)], tag="I.msk")
                    S_act(Esb[:, r, lo * 128:lo * 128 + width], ps, AF.Exp,
                          [(nm2, bk)], [(sk, "E", r)], tag="I.exp")
                wp = w % 2
                Ppo, pnm = (PSB, "PSB") if wp else (PSA, "PSA")
                pok = (pnm, 3)
                po = Ppo[0:65, 1536:2048]
                rbank = Ppo[0:64, 1024:1536]
                rbk = (pnm, 2)
                for j, r in enumerate(avorder):
                    kt_abs, lo, hi, _ = stripes[r]
                    T_mm(po[:, lo * 128:(hi + 1) * 128], vT[:, kt_abs, vsl],
                         Esb[:, r, lo * 128:(hi + 1) * 128],
                         j == 0, j == len(avorder) - 1,
                         [(sk, "vT"), (sk, "E", r)], [pok], tag="I.av")
                V_ts(rrow[wp][64:65, :], po[64:65, :], EPS30,
                     ALU.add, [pok], [(sk, "rr", wp)], tag="I.rsum")
                sc.emit("V", (lambda E, o=rrow[wp][64:65, :],
                              i=rrow[wp][64:65, :]: E.reciprocal(o, i)),
                        reads=[(sk, "rr", wp)], writes=[(sk, "rr", wp)],
                        tag="I.rcp")
                T_mm(rbank, onesf[64:65, 0:64], rrow[wp][64:65, :], True, True,
                     [(sk, "rr", wp), ("c", "onesf")], [rbk], tag="I.rbc")
                S_act(otmp[wp], po[0:64, :], AF.Copy, [pok], [(sk, "ot1", wp)],
                      tag="I.oev")
                V_tt(otmp2[wp], otmp[wp], rbank, ALU.mult,
                     [(sk, "ot1", wp), rbk], [(sk, "ot2", wp)], tag="I.onrm")
                V_copy(oT[r0:r0 + 64, kt4, w * 512:(w + 1) * 512],
                       otmp2[wp], [(sk, "ot2", wp)], [(sk, "oT")], tag="I.ocp")
        if l == 0 and s == 0 and "ot" in dbg:
            for kt in range(4):
                DMA("S", dbg["ot"][kt * 128:(kt + 1) * 128, 0:T], oT[:, kt, :],
                    [(sk, "oT")], [("dbgot", kt)])
        for mt in range(KTD):
            gi = mt % 2
            tb = (l * KTD + mt) * 128
            DMA("S", wo_t[gi][:],
                wo_bf[tb:tb + 128, :].rearrange("p (a c) -> p a c", a=4),
                [("woi", l, mt)], [(sk, "wot", gi)], tag="I.wo.w")
            P, nm = (PSA, "PSA") if mt % 2 == 0 else (PSB, "PSB")
            for kt in range(4):
                for ch in range(4):
                    T_mm(P[:, ch * 512:(ch + 1) * 512],
                         wo_t[gi][:, kt, :],
                         oT[:, kt, ch * 512:(ch + 1) * 512],
                         kt == 0, kt == 3,
                         [(sk, "wot", gi), (sk, "oT")], [(nm, ch)], tag="I.wo")
            kx = ("xtI",)
            DMA("Y", xt, xin_hbm[mt * 128:(mt + 1) * 128, col0:col0 + T],
                [("xin_hbm", mt, s)], [kx], tag="I.wo.x")
            V_stt(xt, P[:, 0:T], 1.0, xt, ALU.mult, ALU.add,
                  [(nm, 0), (nm, 1), (nm, 2), (nm, 3), kx], [kx], tag="I.wo.add")
            DMA("Y", xin_hbm[mt * 128:(mt + 1) * 128, col0:col0 + T], xt,
                [kx], [("xin_hbm", mt, s)], tag="I.wo.out")
        ar.reset()

    def emit_chain(l, s, lk, lg_c, lb_c, b1a_c, b1g_c, b2_c):
        ar.mark()
        sk = ("F", l, s)
        col0 = s * T
        call_fence(
            [("xtF",), (sk, "sq"), (sk, "bmu"), (sk, "bis"),
             (sk, "mur"), (sk, "isr"), (sk, "tmpr")]
            + [(sk, "shb", kt) for kt in range(KTD)]
            + [(sk, "h2", kt) for kt in range(KTD)]
            + [(sk, "h3", mi) for mi in range(22)]
            + [(sk, nm, g) for nm in ("w1a", "w1g", "w2c", "gt")
               for g in range(2)])
        xt = ar.alloc("xtF", [128, T], F32)
        shb = ar.alloc("shb", [128, KTD, T], BF16)
        sq = ar.alloc("sqF", [128, T], BF16)
        bmu = ar.alloc("bmuF", [128, T], BF16)
        bis = ar.alloc("bisF", [128, T], BF16)
        h2 = ar.alloc("h2F", [128, KTD, T], BF16)
        h3 = ar.alloc("h3F", [128, 22, T], BF16)
        mur = ar.alloc_at("murF", [1, T], F32, ar.offsets["h3F"])
        isr = ar.alloc_at("isrF", [1, T], F32, ar.offsets["h3F"] + 8192)
        tmpr = ar.alloc_at("tmprF", [1, T], F32, ar.offsets["h3F"] + 16384)
        sb0 = ar.offsets["shb"]
        w1a = [ar.alloc_at(f"w1a{i}", [128, KTD, 256], BF16, sb0 + i * 4096)
               for i in range(2)]
        w1g = [ar.alloc_at(f"w1g{i}", [128, KTD, 256], BF16, sb0 + 8192 + i * 4096)
               for i in range(2)]
        w2c = [ar.alloc_at(f"w2c{i}", [128, 22, 128], BF16, sb0 + 16384 + i * 5632)
               for i in range(2)]
        gt = ar.alloc_at("gtF", [128, T], BF16, sb0 + 27648)
        for kt in range(KTD):
            kx = ("xtF",)
            DMA("Y", xt, xin_hbm[kt * 128:(kt + 1) * 128, col0:col0 + T],
                [("xin_hbm", kt, s)], [kx], tag="F.x")
            if kt < 4:
                S_act(shb[:, kt, :], xt, AF.Copy, [kx], [(sk, "shb", kt)])
            else:
                memset(shb[:, kt, 0:1], 0.0, [], [(sk, "shb", kt)])
                S_act(shb[:, kt, 1:T], xt[:, 0:T - 1], AF.Copy,
                      [kx], [(sk, "shb", kt)])
            S_act(sq, shb[:, kt, :], AF.Square, [(sk, "shb", kt)], [(sk, "sq")])
            for ch in range(4):
                T_mm(PSA[0:1, ch * 512:(ch + 1) * 512], onesb[:, 0:1],
                     shb[:, kt, ch * 512:(ch + 1) * 512], kt == 0, False,
                     [(sk, "shb", kt), ("c", "ones")], [bankA(ch)], tag="F.sum",
                     sgc=True)
                T_mm(PSA[32:33, ch * 512:(ch + 1) * 512], onesb[:, 0:1],
                     sq[:, ch * 512:(ch + 1) * 512], False, kt == KTD - 1,
                     [(sk, "sq"), ("c", "ones")], [bankA(ch)], tag="F.sumsq",
                     sgc=True)
        V_ts(mur, PSA[0:1, 0:T], 1.0 / D, ALU.mult,
             [bankA(0), bankA(1), bankA(2), bankA(3)], [(sk, "mur")])
        V_ts(isr, PSA[32:33, 0:T], 1.0 / D, ALU.mult,
             [bankA(0), bankA(1), bankA(2), bankA(3)], [(sk, "isr")],
             s2=LNEPS, op1=ALU.add)
        V_tt(tmpr, mur, mur, ALU.mult, [(sk, "mur")], [(sk, "tmpr")])
        V_tt(isr, isr, tmpr, ALU.subtract,
             [(sk, "isr"), (sk, "tmpr")], [(sk, "isr")])
        S_act(isr, isr, AF.Ln, [(sk, "isr")], [(sk, "isr")])
        S_act(isr, isr, AF.Exp, [(sk, "isr")], [(sk, "isr")], scale=-0.5)
        for ch in range(4):
            T_mm(PSA[:, ch * 512:(ch + 1) * 512], onesf[0:1, :],
                 mur[0:1, ch * 512:(ch + 1) * 512], True, True,
                 [(sk, "mur"), ("c", "onesf")], [bankA(ch)], tag="F.bmu")
            T_mm(PSB[:, ch * 512:(ch + 1) * 512], onesf[0:1, :],
                 isr[0:1, ch * 512:(ch + 1) * 512], True, True,
                 [(sk, "isr"), ("c", "onesf")], [bankB(ch)], tag="F.bis")
        S_act(bmu, PSA[:, 0:T], AF.Copy,
              [bankA(0), bankA(1), bankA(2), bankA(3)], [(sk, "bmu")])
        S_act(bis, PSB[:, 0:T], AF.Copy,
              [bankB(0), bankB(1), bankB(2), bankB(3)], [(sk, "bis")])
        for kt in range(KTD):
            V_tt(h2[:, kt, :], shb[:, kt, :], bmu, ALU.subtract,
                 [(sk, "shb", kt), (sk, "bmu")], [(sk, "h2", kt)])
            V_tt(h2[:, kt, :], h2[:, kt, :], bis, ALU.mult,
                 [(sk, "h2", kt), (sk, "bis")], [(sk, "h2", kt)], eng="P")
            S_act(h2[:, kt, :], h2[:, kt, :], AF.Identity,
                  [(sk, "h2", kt), (lk, "lg"), (lk, "lb")], [(sk, "h2", kt)],
                  scale=lg_c[:, kt:kt + 1], bias=lb_c[:, kt:kt + 1])
        for mi in range(6):
            sc.link((sk, "h3", mi), [(sk, "mur"), (sk, "isr"), (sk, "tmpr")])
        shbk = [(sk, "shb", kt) for kt in range(KTD)] + [(sk, "sq")]
        for gi in range(2):
            sc.link((sk, "w1a", gi), shbk)
            sc.link((sk, "w1g", gi), shbk)
            sc.link((sk, "w2c", gi), shbk)
        for hf in range(2):
            sc.link((sk, "gt", hf), shbk)
        memset(h3[32:64, 21, :], 0.0, [], [(sk, "h3", 21)])
        memset(h3[64:128, 21, :], 0.0, [], [(sk, "h3", 21)], eng="P")
        for grp in range(11):
            mi0, gi = grp * 2, grp % 2
            tba = ((l * 2 + 0) * 11 + grp) * 128
            tbg = ((l * 2 + 1) * 11 + grp) * 128
            DMA("S", w1a[gi][:],
                w1_bf[tba:tba + 128, :].rearrange("p (a c) -> p a c", a=KTD),
                [("w1i", l, 0, grp)], [(sk, "w1a", gi)], tag="F.w1a")
            DMA("P", w1g[gi][:],
                w1_bf[tbg:tbg + 128, :].rearrange("p (a c) -> p a c", a=KTD),
                [("w1i", l, 1, grp)], [(sk, "w1g", gi)], tag="F.w1g")
            for mi_ in range(2):
                mi = mi0 + mi_
                mw = min(128, HID - mi * 128)
                for kt in range(KTD):
                    for ch in range(4):
                        T_mm(PSA[0:mw, ch * 512:(ch + 1) * 512],
                             w1a[gi][:, kt, mi_ * 128:mi_ * 128 + mw],
                             h2[:, kt, ch * 512:(ch + 1) * 512],
                             kt == 0, kt == KTD - 1,
                             [(sk, "w1a", gi), (sk, "h2", kt)],
                             [bankA(ch)], tag="F.w1am")
                        T_mm(PSB[0:mw, ch * 512:(ch + 1) * 512],
                             w1g[gi][:, kt, mi_ * 128:mi_ * 128 + mw],
                             h2[:, kt, ch * 512:(ch + 1) * 512],
                             kt == 0, kt == KTD - 1,
                             [(sk, "w1g", gi), (sk, "h2", kt)],
                             [bankB(ch)], tag="F.w1gm")
                for hf in range(2):
                    hc = slice(hf * 1024, (hf + 1) * 1024)
                    S_act(gt[0:mw, hc], PSB[0:mw, hc], AF.Gelu,
                          [bankB(2 * hf), bankB(2 * hf + 1), (lk, "b1")],
                          [(sk, "gt", hf)],
                          bias=b1g_c[0:mw, mi:mi + 1], tag="F.gelu")
                    V_stt(h3[0:mw, mi, hc], PSA[0:mw, hc],
                          b1a_c[0:mw, mi:mi + 1], gt[0:mw, hc], ALU.add,
                          ALU.mult,
                          [bankA(2 * hf), bankA(2 * hf + 1), (sk, "gt", hf),
                           (lk, "b1")],
                          [(sk, "h3", mi)], tag="F.gate")
        for mo in range(KTD):
            gi = mo % 2
            tb2 = (l * KTD + mo) * 128
            DMA(("S", "P")[gi], w2c[gi][:],
                w2_bf[tb2:tb2 + 128, :].rearrange("p (a c) -> p a c", a=22),
                [("w2i", l, mo)], [(sk, "w2c", gi)], tag="F.w2c")
            P, nm = (PSA, "PSA") if mo % 2 == 0 else (PSB, "PSB")
            for kh in range(22):
                for ch in range(4):
                    T_mm(P[:, ch * 512:(ch + 1) * 512],
                         w2c[gi][:, kh, :], h3[:, kh, ch * 512:(ch + 1) * 512],
                         kh == 0, kh == 21,
                         [(sk, "w2c", gi), (sk, "h3", kh)],
                         [(nm, ch)], tag="F.w2m")
            kx = ("xtF",)
            DMA("Y", xt, xin_hbm[mo * 128:(mo + 1) * 128, col0:col0 + T],
                [("xin_hbm", mo, s)], [kx], tag="F.w2x")
            V_stt(xt, P[:, 0:T], b2_c[:, mo:mo + 1], xt, ALU.add, ALU.add,
                  [(nm, 0), (nm, 1), (nm, 2), (nm, 3), kx, (lk, "b2")], [kx],
                  tag="F.w2add")
            DMA("Y", xin_hbm[mo * 128:(mo + 1) * 128, col0:col0 + T], xt,
                [kx], [("xin_hbm", mo, s)], tag="F.w2out")
        ar.reset()

    return emit_inner


_PROG_CACHE = {}


def kernel(L0_attn_out, L0_attn_Ww, L0_attn_Wr, L0_attn_decay, L0_attn_log_alpha,
           L0_ffn_wqkv, L0_ffn_q_scale, L0_ffn_k_scale, L0_ffn_wo, L0_ffn_ln_g,
           L0_ffn_ln_b, L0_ffn_w1, L0_ffn_b1, L0_ffn_w2, L0_ffn_b2):
    x0 = np.asarray(L0_attn_out, dtype=np.float32)
    shards = []
    for c in range(NCORES):
        sh = x0[c * SEQ:(c + 1) * SEQ]
        sh = np.ascontiguousarray(sh.transpose(2, 0, 1).reshape(D, M))
        shards.append(sh)
    weights = {
        "Ww": np.ascontiguousarray(L0_attn_Ww, np.float32),
        "Wr": np.ascontiguousarray(L0_attn_Wr, np.float32),
        "decay": np.ascontiguousarray(L0_attn_decay, np.float32).reshape(1, DN_HEADS),
        "log_alpha": np.ascontiguousarray(L0_attn_log_alpha, np.float32).reshape(1, DN_HEADS),
        "wqkv": np.ascontiguousarray(L0_ffn_wqkv, np.float32).reshape(IN_DEPTH * D, 3 * INNER),
        "q_scale": np.ascontiguousarray(L0_ffn_q_scale, np.float32),
        "k_scale": np.ascontiguousarray(L0_ffn_k_scale, np.float32),
        "wo": np.ascontiguousarray(L0_ffn_wo, np.float32).reshape(IN_DEPTH * INNER, D),
        "ln_g": np.ascontiguousarray(L0_ffn_ln_g, np.float32),
        "ln_b": np.ascontiguousarray(L0_ffn_ln_b, np.float32),
        "w1": np.ascontiguousarray(L0_ffn_w1, np.float32).reshape(IN_DEPTH * D, HID2),
        "b1": np.ascontiguousarray(L0_ffn_b1, np.float32),
        "w2": np.ascontiguousarray(L0_ffn_w2, np.float32).reshape(IN_DEPTH * HID, D),
        "b2": np.ascontiguousarray(L0_ffn_b2, np.float32),
    }
    consts = {"c_" + k: np.ascontiguousarray(v) for k, v in CONSTS.items()}

    key = "prog"
    if key not in _PROG_CACHE:
        _PROG_CACHE[key] = build_program()
    nc, _es = _PROG_CACHE[key]

    in_maps = []
    for c in range(NCORES):
        m = {"x0": shards[c]}
        m.update(weights)
        m.update(consts)
        in_maps.append(m)
    res = run_bass_kernel_spmd(nc, in_maps, core_ids=list(range(NCORES)))
    outs = []
    for c in range(NCORES):
        y = res.results[c]["out"]
        y = y.reshape(D, SEQ, T).transpose(1, 2, 0)
        outs.append(y)
    full = np.concatenate(outs, axis=0).astype(np.float32)
    if DEBUG_DUMP:
        kernel.debug_results = res.results
    return full


# version 17
# speedup vs baseline: 1.1575x; 1.1575x over previous
import os
import numpy as np
import ml_dtypes
from contextlib import ExitStack
import concourse.bass as bass
import concourse.mybir as mybir
from concourse.bass_utils import run_bass_kernel_spmd

F32 = mybir.dt.float32
BF16 = mybir.dt.bfloat16
AF = mybir.ActivationFunctionType
ALU = mybir.AluOpType
AXX = mybir.AxisListType.X

DEBUG_DUMP = [s for s in os.environ.get("MK_DEBUG_DUMP", "").split(",") if s]
PHASE_LIMIT = int(os.environ.get("MK_PHASE_LIMIT", "99"))

B, T, D = 16, 2048, 1024
NCORES = 8
SEQ = B // NCORES
M = SEQ * T
DN_HEADS, DN_D, DN_C = 4, 256, 64
DN_CHUNKS = T // DN_C
IN_DEPTH, IN_HEADS, IN_DH, WINDOW = 2, 8, 64, 512
INNER = IN_HEADS * IN_DH
HID, HID2 = 2730, 5460
NW = T // WINDOW
N_PAIR = SEQ * DN_HEADS
QK_SCALE = 8.0
KTD = D // 128
HIDP = 22 * 128
LNEPS = float(np.float32(1e-5))


class Sched:
    COMPUTE = ("T", "V", "S", "P", "Y")
    QUEUE_OF = {"Y": "QY", "S": "QS", "P": "QP"}
    POOL = 24

    def __init__(self):
        self.ops = []
        self.eng_ops = {e: [] for e in self.COMPUTE}
        self.trk_count = {t: 0 for t in ("T", "V", "S", "P", "Y")}
        self.q_count = {q: 0 for q in ("QY", "QS", "QP")}
        for q in ("QY", "QS", "QP"):
            for s in range(self.POOL):
                self.trk_count[(q, s)] = 0
        self.last_w = {}
        self.readers = {}

    def emit(self, eng, fn, reads=(), writes=(), dma=False, tag=""):
        if eng == "P" and not dma and not os.environ.get("MK_USE_GP"):
            eng = "V"
        if dma:
            q = self.QUEUE_OF[eng]
            slot = self.q_count[q] % self.POOL
            self.q_count[q] += 1
            tracker = (q, slot)
        else:
            tracker = eng
        seq = self.trk_count[tracker]
        self.trk_count[tracker] += 1
        deps = set()
        for r in reads:
            w = self.last_w.get(r)
            if w is not None:
                deps.add(w)
        for w_ in writes:
            lw = self.last_w.get(w_)
            if lw is not None:
                deps.add(lw)
            deps.update(self.readers.get(w_, ()))
        idx = len(self.ops)
        op = dict(idx=idx, eng=eng, fn=fn, tracker=tracker, seq=seq,
                  deps=sorted(deps), dma=dma, tag=tag)
        self.ops.append(op)
        self.eng_ops[eng].append(op)
        for r in reads:
            self.readers.setdefault(r, []).append(idx)
        for w_ in writes:
            self.last_w[w_] = idx
            self.readers[w_] = []
        return idx

    def link(self, new_key, old_keys):
        deps = []
        for k in old_keys:
            w = self.last_w.get(k)
            if w is not None:
                deps.append(w)
            deps.extend(self.readers.get(k, ()))
        self.readers.setdefault(new_key, []).extend(sorted(set(deps)))

    def barrier(self):
        lasts = {}
        for op in self.ops:
            lasts[op["tracker"]] = op["idx"]
        dep_idxs = sorted(lasts.values())
        for e in self.COMPUTE:
            seq = self.trk_count[e]
            self.trk_count[e] += 1
            idx = len(self.ops)
            op = dict(idx=idx, eng=e, fn=(lambda E: E.nop()), tracker=e,
                      seq=seq, deps=list(dep_idxs), dma=False, tag="barrier")
            self.ops.append(op)
            self.eng_ops[e].append(op)

    def resolve(self):
        clocks = {}
        cur = {e: {} for e in self.COMPUTE}
        self.need_inc = set()
        self.op_waits = {}
        for op in self.ops:
            e = op["eng"]
            clk = cur[e]
            waits = []
            changed = False
            for d in op["deps"]:
                p = self.ops[d]
                pt, ps = p["tracker"], p["seq"]
                implicit = (not p["dma"]) and (not op["dma"]) and pt == e == "T"
                if implicit:
                    for t, s in clocks[d].items():
                        if clk.get(t, -1) < s:
                            if not changed:
                                clk = dict(clk); changed = True
                            clk[t] = s
                    continue
                if clk.get(pt, -1) >= ps:
                    continue
                waits.append(d)
                if not changed:
                    clk = dict(clk); changed = True
                clk[pt] = ps
                self.need_inc.add(d)
                for t, s in clocks[d].items():
                    if clk.get(t, -1) < s:
                        clk[t] = s
            cur[e] = clk
            done = dict(clk)
            done[op["tracker"]] = op["seq"]
            clocks[op["idx"]] = done
            self.op_waits[op["idx"]] = waits
        for op in self.ops:
            if op["dma"]:
                self.need_inc.add(op["idx"])
        self.incnum = {}
        cnt = {t: 0 for t in self.trk_count}
        for op in self.ops:
            if op["idx"] in self.need_inc:
                cnt[op["tracker"]] += 1
                self.incnum[op["idx"]] = cnt[op["tracker"]]
        self.final_counts = dict(cnt)

    def materialize(self, nc, es):
        self.resolve()
        sems = {t: es.enter_context(nc.semaphore(f"sem_{i}"))
                for i, t in enumerate(self.trk_count)}
        block = es.enter_context(nc.Block())

        def run_stream(E, name):
            for op in self.eng_ops[name]:
                for d in self.op_waits[op["idx"]]:
                    p = self.ops[d]
                    val = self.incnum[d] * (16 if p["dma"] else 1)
                    E.wait_ge(sems[p["tracker"]], val)
                if op["dma"] and op["seq"] > 0:
                    E.wait_ge(sems[op["tracker"]], 16 * op["seq"])
                try:
                    inst = op["fn"](E)
                except Exception:
                    print(f"EMIT FAIL at op {op['idx']} eng={op['eng']} "
                          f"tag={op['tag']}")
                    raise
                if op["idx"] in self.need_inc:
                    inst.then_inc(sems[op["tracker"]], 16 if op["dma"] else 1)

        block.tensor(lambda E: run_stream(E, "T"))
        block.vector(lambda E: run_stream(E, "V"))
        block.scalar(lambda E: run_stream(E, "S"))
        block.gpsimd(lambda E: run_stream(E, "P"))

        def sync_fn(E):
            run_stream(E, "Y")
            for t, n in self.final_counts.items():
                if isinstance(t, tuple) and n > 0:
                    E.wait_ge(sems[t], 16 * n)
        block.sync(sync_fn)


class Arena:
    def __init__(self, nc):
        self.nc = nc
        self.off = (nc.sbuf_base + 31) & ~31
        self.limit = nc.sbuf_top - 64
        self.marks = []
        self.n = 0

    def alloc(self, name, shape, dtype):
        esz = 2 if dtype == BF16 else 4
        per_part = int(np.prod(shape[1:])) * esz
        off = (self.off + 31) & ~31
        assert off + per_part <= self.limit, \
            f"SBUF overflow: {name} needs {per_part}B at {off} (limit {self.limit})"
        self.n += 1
        t = self.nc.alloc_sbuf_tensor_at(f"{name}_{self.n}", list(shape), dtype,
                                         offset=off)
        self.off = off + per_part
        self.offsets = getattr(self, "offsets", {})
        self.offsets[name] = off
        return t.ap()

    def mark(self):
        self.marks.append(self.off)

    def reset(self):
        self.off = self.marks.pop()

    def alloc_at(self, name, shape, dtype, offset):
        self.n += 1
        t = self.nc.alloc_sbuf_tensor_at(f"{name}_{self.n}", list(shape), dtype,
                                         offset=offset)
        return t.ap()


def _host_constants():
    c = {}
    c["eye128"] = np.eye(128, dtype=ml_dtypes.bfloat16)
    c["eye64b"] = np.eye(DN_C, dtype=ml_dtypes.bfloat16)
    c["ones_b"] = np.ones((128, 128), dtype=ml_dtypes.bfloat16)
    c["ones_f"] = np.ones((128, 128), dtype=np.float32)
    p = np.arange(DN_C)
    diffs = np.maximum(p[:, None] - p[None, :], 0).astype(np.float32)
    c["diffs64"] = diffs
    c["diffs64T"] = np.ascontiguousarray(diffs.T)
    strict = np.tril(np.ones((DN_C, DN_C), np.float32), -1)
    c["strictlow_neg"] = -strict
    c["strictup_neg"] = -np.ascontiguousarray(strict.T)
    c["strictup_pos"] = np.ascontiguousarray(strict.T)
    c["pcol"] = p.astype(np.float32).reshape(DN_C, 1)
    c["ptailcol"] = (DN_C - 1 - p).astype(np.float32).reshape(DN_C, 1)
    c["prow"] = p.astype(np.float32).reshape(1, DN_C)
    r = np.arange(128)[:, None]
    q = np.arange(128)[None, :]
    NEG = np.float32(-30000.0)
    c["maskA"] = np.where(q > r, NEG, 0.0).astype(np.float32)
    c["maskB"] = np.where(q < r, NEG, 0.0).astype(np.float32)
    half = np.arange(0, IN_DH, 2, dtype=np.float32)
    inv_freq = 1.0 / 10000.0 ** (half / IN_DH)
    t = np.arange(T, dtype=np.float32)
    freqs = t[:, None] * inv_freq[None, :]
    freqs = np.concatenate([freqs, freqs], axis=-1)
    base = (half + 0.4 * IN_DH) / (1.4 * IN_DH)
    power = (t - T // 2) / float(WINDOW // 2)
    scale = base[None, :] ** power[:, None]
    scale = np.concatenate([scale, scale], axis=-1)
    cos, sin = np.cos(freqs), np.sin(freqs)
    sgn = np.where(np.arange(IN_DH) < IN_DH // 2, -1.0, 1.0).astype(np.float32)
    cs_q = (cos * scale).T.astype(np.float32)
    ss_q = (sin * scale).T.astype(np.float32) * sgn[:, None]
    cs_k = (cos / scale).T.astype(np.float32)
    ss_k = (sin / scale).T.astype(np.float32) * sgn[:, None]
    c["cs_q"] = np.tile(cs_q * QK_SCALE, (2, 1))
    c["ss_q"] = np.tile(ss_q * QK_SCALE, (2, 1))
    c["cs_k"] = np.tile(cs_k, (2, 1))
    c["ss_k"] = np.tile(ss_k, (2, 1))
    P1 = np.zeros((64, 64), np.float32)
    for d_ in range(64):
        P1[(d_ + 32) % 64, d_] = 1.0
    Psw = np.zeros((128, 128), np.float32)
    Psw[:64, :64] = P1
    Psw[64:, 64:] = P1
    c["Pswap"] = Psw.astype(ml_dtypes.bfloat16)
    ind = np.zeros((KTD, 128, DN_HEADS), np.float32)
    sel = np.zeros((KTD, DN_HEADS, 128), np.float32)
    for kt in range(KTD):
        ind[kt, :, kt // 2] = 1.0
        sel[kt, kt // 2, :] = 1.0
    c["ind_dn"] = ind.transpose(1, 0, 2).reshape(128, KTD * DN_HEADS).astype(ml_dtypes.bfloat16)
    c["sel_dn"] = sel.transpose(1, 0, 2).reshape(DN_HEADS, KTD * 128).astype(np.float32)
    ind2 = np.zeros((4, 128, IN_HEADS), np.float32)
    sel2 = np.zeros((4, IN_HEADS, 128), np.float32)
    for kt in range(4):
        ind2[kt, 0:64, 2 * kt] = 1.0
        ind2[kt, 64:128, 2 * kt + 1] = 1.0
        sel2[kt, 2 * kt, 0:64] = 1.0
        sel2[kt, 2 * kt + 1, 64:128] = 1.0
    c["ind_in"] = ind2.transpose(1, 0, 2).reshape(128, 4 * IN_HEADS).astype(ml_dtypes.bfloat16)
    c["sel_in"] = sel2.transpose(1, 0, 2).reshape(IN_HEADS, 4 * 128).astype(np.float32)
    return c


CONSTS = _host_constants()


def build_program():
    nc = bass.Bass("TRN2", target_bir_lowering=False, debug=False)
    sc = Sched()
    es = ExitStack()
    es.__enter__()

    H = {}
    H["x0"] = nc.dram_tensor("x0", [D, M], F32, kind="ExternalInput").ap()
    H["Ww"] = nc.dram_tensor("Ww", [D, D], F32, kind="ExternalInput").ap()
    H["Wr"] = nc.dram_tensor("Wr", [D, D], F32, kind="ExternalInput").ap()
    H["decay"] = nc.dram_tensor("decay", [1, DN_HEADS], F32, kind="ExternalInput").ap()
    H["log_alpha"] = nc.dram_tensor("log_alpha", [1, DN_HEADS], F32, kind="ExternalInput").ap()
    H["wqkv"] = nc.dram_tensor("wqkv", [IN_DEPTH * D, 3 * INNER], F32, kind="ExternalInput").ap()
    H["q_scale"] = nc.dram_tensor("q_scale", [IN_DEPTH, IN_DH], F32, kind="ExternalInput").ap()
    H["k_scale"] = nc.dram_tensor("k_scale", [IN_DEPTH, IN_DH], F32, kind="ExternalInput").ap()
    H["wo"] = nc.dram_tensor("wo", [IN_DEPTH * INNER, D], F32, kind="ExternalInput").ap()
    H["ln_g"] = nc.dram_tensor("ln_g", [IN_DEPTH, D], F32, kind="ExternalInput").ap()
    H["ln_b"] = nc.dram_tensor("ln_b", [IN_DEPTH, D], F32, kind="ExternalInput").ap()
    H["w1"] = nc.dram_tensor("w1", [IN_DEPTH * D, HID2], F32, kind="ExternalInput").ap()
    H["b1"] = nc.dram_tensor("b1", [IN_DEPTH, HID2], F32, kind="ExternalInput").ap()
    H["w2"] = nc.dram_tensor("w2", [IN_DEPTH * HID, D], F32, kind="ExternalInput").ap()
    H["b2"] = nc.dram_tensor("b2", [IN_DEPTH, D], F32, kind="ExternalInput").ap()
    CK = {}
    for k, v in CONSTS.items():
        dt = BF16 if v.dtype == ml_dtypes.bfloat16 else F32
        CK[k] = nc.dram_tensor("c_" + k, list(v.shape), dt, kind="ExternalInput").ap()
    out_h = nc.dram_tensor("out", [D, M], F32, kind="ExternalOutput").ap()

    v_hbm = nc.dram_tensor("v_hbm", [M, D], BF16).ap()
    ln1_hbm = nc.dram_tensor("ln1_hbm", [D, M], F32).ap()
    a_hbm = nc.dram_tensor("a_hbm", [D, M], F32).ap()
    x1_hbm = nc.dram_tensor("x1_hbm", [D, M], F32).ap()
    xin_hbm = nc.dram_tensor("xin_hbm", [D, M], F32).ap()
    wqkv_bf = nc.dram_tensor("wqkv_bf", [IN_DEPTH * 12 * 128, KTD * 128], BF16).ap()
    wo_bf = nc.dram_tensor("wo_bf", [IN_DEPTH * KTD * 128, 4 * 128], BF16).ap()
    w1_bf = nc.dram_tensor("w1_bf", [IN_DEPTH * 2 * 11 * 128, KTD * 256], BF16).ap()
    w2_bf = nc.dram_tensor("w2_bf", [IN_DEPTH * KTD * 128, 22 * 128], BF16).ap()

    dbg = {}
    for name in DEBUG_DUMP:
        shape, dt = {
            "wwt": ([D, D], BF16),
            "ln1": ([D, M], F32), "v": ([M, D], BF16), "rk": ([D, M], BF16),
            "o": ([D, M], BF16), "a": ([D, M], F32), "x1": ([D, M], F32),
            "xin": ([D, M], F32), "xwo": ([D, M], F32), "xffn": ([D, M], F32),
            "qf": ([INNER, M], BF16), "kf": ([INNER, M], BF16),
            "vv": ([INNER, M], BF16), "ot": ([INNER, M], BF16),
            "h2": ([D, M], BF16),
        }[name]
        dbg[name] = nc.dram_tensor("dbg_" + name, shape, dt,
                                   kind="ExternalOutput").ap()

    ar = Arena(nc)

    eye = ar.alloc("eye", [128, 128], BF16)
    onesb = ar.alloc("onesb", [128, 128], BF16)
    onesf = ar.alloc("onesf", [128, 128], F32)
    C = {}

    def load_const(name, shape, dtype):
        t = ar.alloc("c" + name, shape, dtype)
        sc.emit("Y", (lambda E, d=t, s=CK[name]: E.dma_start(out=d, in_=s)),
                writes=[("c", name)], dma=True, tag="const")
        C[name] = t
        return t

    sc.emit("Y", lambda E: E.dma_start(out=eye, in_=CK["eye128"]),
            writes=[("c", "eye")], dma=True)
    sc.emit("Y", lambda E: E.dma_start(out=onesb, in_=CK["ones_b"]),
            writes=[("c", "ones")], dma=True)
    sc.emit("Y", lambda E: E.dma_start(out=onesf, in_=CK["ones_f"]),
            writes=[("c", "onesf")], dma=True)
    for nm in ("eye64b", "diffs64", "diffs64T", "strictlow_neg", "strictup_neg",
               "strictup_pos", "pcol", "ptailcol", "prow", "ind_dn", "sel_dn",
               "Pswap", "ind_in", "sel_in", "maskA", "maskB"):
        load_const(nm, list(CONSTS[nm].shape), BF16 if CONSTS[nm].dtype == ml_dtypes.bfloat16 else F32)

    PSA = es.enter_context(nc.psum_tensor("PSA", [128, 2048], F32)).ap()
    PSB = es.enter_context(nc.psum_tensor("PSB", [128, 2048], F32)).ap()

    def bankA(i):
        return ("PSA", i)

    def bankB(i):
        return ("PSB", i)

    def T_mm(out, lhsT, rhs, st, sp, reads, wbanks, tag="mm", tp=None, sgc=False):
        sc.emit("T", (lambda E, o=out, l=lhsT, r=rhs, a=st, b=sp, t=tp, g=sgc:
                      E.matmul(o, l, r, start=a, stop=b, tile_position=t,
                               skip_group_check=g)),
                reads=list(reads), writes=list(wbanks), tag=tag)

    def T_tr(out, in_, ident, reads, wbanks, tag="tr"):
        sc.emit("T", (lambda E, o=out, i=in_, d=ident: E.transpose(o, i, d)),
                reads=list(reads), writes=list(wbanks), tag=tag)

    def S_act(out, in_, func, reads, writes, bias=0.0, scale=1.0, tag="act"):
        sc.emit("S", (lambda E, o=out, i=in_, f=func, b=bias, s=scale:
                      E.activation(out=o, in_=i, func=f, bias=b, scale=s)),
                reads=list(reads), writes=list(writes), tag=tag)

    def V_tt(out, in0, in1, op, reads, writes, eng="V", tag="tt"):
        sc.emit(eng, (lambda E, o=out, a=in0, b=in1, p=op:
                      E.tensor_tensor(o, a, b, op=p)),
                reads=list(reads), writes=list(writes), tag=tag)

    def V_stt(out, in0, scalar, in1, op0, op1, reads, writes, tag="stt"):
        sc.emit("V", (lambda E, o=out, a=in0, s=scalar, b=in1, p0=op0, p1=op1:
                      E.scalar_tensor_tensor(o, a, s, b, op0=p0, op1=p1)),
                reads=list(reads), writes=list(writes), tag=tag)

    def V_ts(out, in0, s1, op0, reads, writes, s2=None, op1=ALU.bypass, eng="V", tag="ts"):
        sc.emit(eng, (lambda E, o=out, a=in0, x1=s1, x2=s2, p0=op0, p1=op1:
                      E.tensor_scalar(o, a, x1, x2, op0=p0, op1=p1)),
                reads=list(reads), writes=list(writes), tag=tag)

    def DMA(eng, out, in_, reads, writes, tag="dma"):
        sc.emit(eng, (lambda E, o=out, i=in_:
                      E.dma_start(out=o, in_=i, allow_slow_non_contiguous=True)),
                reads=list(reads), writes=list(writes), dma=True, tag=tag)

    def V_copy(out, in_, reads, writes, eng="V", tag="cp"):
        sc.emit(eng, (lambda E, o=out, i=in_: E.tensor_copy(o, i)),
                reads=list(reads), writes=list(writes), tag=tag)

    def V_red(out, in_, op, reads, writes, tag="red", absval=False):
        sc.emit("V", (lambda E, o=out, i=in_, p=op, a=absval:
                      E.tensor_reduce(o, i, axis=AXX, op=p,
                                      apply_absolute_value=a)),
                reads=list(reads), writes=list(writes), tag=tag)

    def memset(ap, val, reads, writes, eng="V"):
        sc.emit(eng, (lambda E, a=ap, v=val: E.memset(a, v)),
                reads=list(reads), writes=list(writes), tag="memset")

    ztile = ar.alloc("ztile", [128, D], BF16)
    memset(ztile, 0.0, [], [("ztile",)])
    for l in range(IN_DEPTH if not os.environ.get("MK_SKIP_IMG") else 0):
        for mt in range(12):
            tb = (l * 12 + mt) * 128
            DMA("P", wqkv_bf[tb:tb + 128, :].rearrange("p (a c) -> p a c", a=KTD),
                H["wqkv"][l * D:(l + 1) * D, mt * 128:(mt + 1) * 128]
                .rearrange("(a p) c -> p a c", p=128),
                [], [("wqi", l, mt)], tag="img_wq")
        for mt in range(KTD):
            tb = (l * KTD + mt) * 128
            DMA("P", wo_bf[tb:tb + 128, :].rearrange("p (a c) -> p a c", a=4),
                H["wo"][l * INNER:(l + 1) * INNER, mt * 128:(mt + 1) * 128]
                .rearrange("(a p) c -> p a c", p=128),
                [], [("woi", l, mt)], tag="img_wo")
        for ag in range(2):
            off = ag * HID
            for grp in range(11):
                tb = ((l * 2 + ag) * 11 + grp) * 128
                gw = min(256, HID - grp * 256)
                if gw == 256:
                    DMA("P", w1_bf[tb:tb + 128, :]
                        .rearrange("p (a c) -> p a c", a=KTD),
                        H["w1"][l * D:(l + 1) * D,
                                off + grp * 256:off + grp * 256 + 256]
                        .rearrange("(a p) c -> p a c", p=128),
                        [], [("w1i", l, ag, grp)], tag="img_w1")
                else:
                    for kt in range(KTD):
                        DMA("P", w1_bf[tb:tb + 128, kt * 256:kt * 256 + gw],
                            H["w1"][l * D + kt * 128:l * D + kt * 128 + 128,
                                    off + grp * 256:off + grp * 256 + gw],
                            [], [("w1i", l, ag, grp)], tag="img_w1t")
                        DMA("Y", w1_bf[tb:tb + 128, kt * 256 + gw:(kt + 1) * 256],
                            ztile[:, 0:256 - gw], [("ztile",)],
                            [("w1i", l, ag, grp)], tag="img_w1z")
        for mo in range(KTD):
            tb = (l * KTD + mo) * 128
            DMA("P", w2_bf[tb:tb + 128, 0:21 * 128]
                .rearrange("p (a c) -> p a c", a=21),
                H["w2"][l * HID:l * HID + 2688, mo * 128:(mo + 1) * 128]
                .rearrange("(a p) c -> p a c", p=128),
                [], [("w2i", l, mo)], tag="img_w2")
            DMA("P", w2_bf[tb:tb + 42, 21 * 128:22 * 128],
                H["w2"][l * HID + 2688:l * HID + 2730, mo * 128:(mo + 1) * 128],
                [], [("w2i", l, mo)], tag="img_w2t")
            DMA("Y", w2_bf[tb + 42:tb + 128, 21 * 128:22 * 128],
                ztile[0:86, 0:128], [("ztile",)], [("w2i", l, mo)], tag="img_w2z")

    grow = ar.alloc("grow", [1, 512], F32)
    decay_sb = ar.alloc("decay_sb", [1, 128], F32)
    lalpha_sb = ar.alloc("lalpha_sb", [1, 128], F32)
    DMA("Y", decay_sb[:, 0:DN_HEADS], H["decay"], [], [("dec",)])
    DMA("Y", decay_sb[:, DN_HEADS:2 * DN_HEADS], H["decay"], [], [("dec",)])
    DMA("Y", lalpha_sb[:, 0:DN_HEADS], H["log_alpha"], [], [("lal",)])
    DMA("Y", lalpha_sb[:, DN_HEADS:2 * DN_HEADS], H["log_alpha"], [], [("lal",)])
    S_act(grow[:, 384:512], decay_sb, AF.Exp, [("dec",)], [("grow", 3)], scale=-1.0)
    V_ts(grow[:, 384:512], grow[:, 384:512], 1.0, ALU.add, [("grow", 3)], [("grow", 3)])
    S_act(grow[:, 128:256], grow[:, 384:512], AF.Ln, [("grow", 3)], [("grow", 1)],
          scale=1.0)
    V_ts(grow[:, 128:256], grow[:, 128:256], -1.0, ALU.mult, [("grow", 1)], [("grow", 1)])
    S_act(grow[:, 0:128], grow[:, 128:256], AF.Exp, [("grow", 1)], [("grow", 0)])
    S_act(grow[:, 256:384], lalpha_sb, AF.Exp, [("lal",)], [("grow", 2)])

    lg64 = [ar.alloc(f"lg64_{h}", [64, 1], F32) for h in range(DN_HEADS)]
    lg128 = [ar.alloc(f"lg128_{h}", [128, 1], F32) for h in range(DN_HEADS)]
    gp_col = [ar.alloc(f"gp_{h}", [64, 1], F32) for h in range(DN_HEADS)]
    gtail_col = [ar.alloc(f"gt_{h}", [64, 1], F32) for h in range(DN_HEADS)]
    gC_col = [ar.alloc(f"gC_{h}", [128, 1], F32) for h in range(DN_HEADS)]
    al_col = [ar.alloc(f"al_{h}", [128, 1], F32) for h in range(DN_HEADS)]
    gtile = [ar.alloc(f"gtile_{h}", [128, 64], BF16) for h in range(DN_HEADS)]
    mlow_neg = [ar.alloc(f"mlo_{h}", [64, 64], BF16) for h in range(DN_HEADS)]
    mup_neg = [ar.alloc(f"mun_{h}", [64, 64], BF16) for h in range(DN_HEADS)]
    mup_pos = [ar.alloc(f"mup_{h}", [64, 64], BF16) for h in range(DN_HEADS)]

    def emit_dn_consts():
        etmp = ar.alloc("etmp", [64, 64], F32)
        rtmp = ar.alloc("rtmp", [1, 64], F32)
        for h in range(DN_HEADS):
            if os.environ.get("MK_SERIAL"):
                sc.barrier()
            T_mm(PSA[0:128, 0:1], onesf[0:1, 0:128], grow[:, 128 + h:129 + h],
                 True, True, [("c", "onesf"), ("grow", 1)], [bankA(0)], tag="lgbc")
            V_copy(lg128[h], PSA[0:128, 0:1], [bankA(0)], [("lg128", h)])
            V_copy(lg64[h], PSA[0:64, 0:1], [bankA(0)], [("lg64", h)])
            T_mm(PSA[0:128, 512:513], onesf[0:1, 0:128], grow[:, 256 + h:257 + h],
                 True, True, [("c", "onesf"), ("grow", 2)], [bankA(1)], tag="albc")
            V_copy(al_col[h], PSA[0:128, 512:513], [bankA(1)], [("al", h)])
            S_act(gp_col[h], C["pcol"][0:64, :], AF.Exp, [("c", "pcol"), ("lg64", h)],
                  [("gp", h)], scale=lg64[h])
            S_act(gtail_col[h], C["ptailcol"][0:64, :], AF.Exp,
                  [("c", "ptailcol"), ("lg64", h)], [("gt", h)], scale=lg64[h])
            sc.emit("S", (lambda E, o=gC_col[h], i=lg128[h]:
                          E.activation(out=o, in_=i, func=AF.Exp, scale=float(DN_C))),
                    reads=[("lg128", h)], writes=[("gC", h)])
            V_ts(rtmp, C["prow"][0:1, 0:64], grow[:, 128 + h:129 + h], ALU.mult,
                 [("c", "prow"), ("grow", 1)], [("rtmp",)])
            S_act(rtmp, rtmp, AF.Exp, [("rtmp",)], [("rtmp",)])
            T_mm(PSA[0:128, 1024:1088], onesf[0:1, 0:128], rtmp[0:1, 0:64],
                 True, True, [("c", "onesf"), ("rtmp",)], [bankA(2)], tag="gtile")
            S_act(gtile[h], PSA[0:128, 1024:1088], AF.Copy, [bankA(2)], [("gtile", h)])
            S_act(etmp, C["diffs64"][0:64, :], AF.Exp, [("c", "diffs64"), ("lg64", h)],
                  [("etmp",)], scale=lg64[h])
            V_tt(mlow_neg[h], etmp, C["strictlow_neg"][0:64, :], ALU.mult,
                 [("etmp",), ("c", "strictlow_neg")], [("mlo", h)])
            S_act(etmp, C["diffs64T"][0:64, :], AF.Exp, [("c", "diffs64T"), ("lg64", h)],
                  [("etmp",)], scale=lg64[h])
            V_tt(mup_neg[h], etmp, C["strictup_neg"][0:64, :], ALU.mult,
                 [("etmp",), ("c", "strictup_neg")], [("mun", h)])
            V_tt(mup_pos[h], etmp, C["strictup_pos"][0:64, :], ALU.mult,
                 [("etmp",), ("c", "strictup_pos")], [("mup", h)])

    emit_dn_consts()
    if os.environ.get("MK_SERIAL"):
        sc.barrier()
    if os.environ.get("MK_DEBUG_DN"):
        cdbg = nc.dram_tensor("cdbg", [128, 1400], F32, kind="ExternalOutput").ap()
        for h in range(DN_HEADS):
            b = h * 200
            DMA("S", cdbg[0:64, b:b + 64].bitcast(BF16)[:, 0:64], mlow_neg[h],
                [("mlo", h)], [("cdbg", h, 0)])
            DMA("S", cdbg[0:128, b + 64:b + 96].bitcast(BF16)[:, 0:64], gtile[h],
                [("gtile", h)], [("cdbg", h, 1)])
            DMA("S", cdbg[0:64, b + 96:b + 97], lg64[h], [("lg64", h)], [("cdbg", h, 2)])
            DMA("S", cdbg[0:64, b + 97:b + 98], gp_col[h], [("gp", h)], [("cdbg", h, 3)])
            DMA("S", cdbg[0:64, b + 98:b + 99], gtail_col[h], [("gt", h)], [("cdbg", h, 4)])
            DMA("S", cdbg[0:128, b + 99:b + 100], gC_col[h], [("gC", h)], [("cdbg", h, 5)])
            DMA("S", cdbg[0:128, b + 100:b + 101], al_col[h], [("al", h)], [("cdbg", h, 6)])
        DMA("S", cdbg[0:1, 1000:1400].rearrange("a (b c) -> a b c", b=1), grow[:, 0:400].rearrange("a (b c) -> a b c", b=1), [("grow", 0), ("grow", 1), ("grow", 2)],
            [("cdbg", "grow")])

    o_hbm = nc.dram_tensor("o_hbm", [D, M], BF16).ap()
    ar.mark()
    rk = ar.alloc("rk", [128, KTD, M], BF16)

    def emit_wT(dst, src_h, dkey):
        wtile = ar.alloc("wtile" + dkey, [128, D], F32)
        wtb = ar.alloc("wtb" + dkey, [128, D], BF16)
        for ot in range(KTD):
            DMA("Y", wtile, src_h[ot * 128:(ot + 1) * 128, :], [],
                [(dkey, "wtile")], tag="wT")
            S_act(wtb, wtile, AF.Copy, [(dkey, "wtile")], [(dkey, "wtb")])
            for it in range(KTD):
                pb = PSA[:, (it % 4) * 512:(it % 4) * 512 + 128].bitcast(BF16)[:, 0:128]
                T_tr(pb, wtb[:, it * 128:(it + 1) * 128], eye,
                     [(dkey, "wtb"), ("c", "eye")], [bankA(it % 4)])
                S_act(dst[:, it, ot * 128:(ot + 1) * 128], pb, AF.Copy,
                      [bankA(it % 4)], [(dkey, "all")])

    def emit_phaseA():
        ar.mark()
        WwT = ar.alloc("WwT", [128, KTD, D], BF16)
        emit_wT(WwT, H["Ww"], "WwT")
        x0t = ar.alloc("x0t", [128, 2048], F32)
        sq = ar.alloc("sqA", [128, 2048], BF16)
        ln1b = ar.alloc("ln1b", [128, KTD, 2048], BF16)
        ln1f = ar.alloc("ln1f", [128, 2048], F32)
        bmu = ar.alloc("bmu", [128, 2048], F32)
        bis = ar.alloc("bis", [128, 2048], F32)
        mu_r = ar.alloc("mu_rA", [1, 2048], F32)
        is_r = ar.alloc("is_rA", [1, 2048], F32)
        tmp_r = ar.alloc("tmp_rA", [1, 2048], F32)
        row4 = ar.alloc("row4", [4, 2048], F32)
        rnb = ar.alloc_at("rnb", [128, 2048], F32, ar.offsets["bmu"])
        vtb = ar.alloc_at("vtb", [128, 1024], BF16, ar.offsets["bis"])

        def ln_rows(mu_ap, is_ap, tmp_ap, psum_sum, psum_sq, rkeys, wk_mu, wk_is):
            V_ts(mu_ap, psum_sum, 1.0 / D, ALU.mult, rkeys, [wk_mu])
            V_ts(is_ap, psum_sq, 1.0 / D, ALU.mult, rkeys, [wk_is],
                 s2=LNEPS, op1=ALU.add)
            V_tt(tmp_ap, mu_ap, mu_ap, ALU.mult, [wk_mu], [("tmp_r",)])
            V_tt(is_ap, is_ap, tmp_ap, ALU.subtract, [wk_is, ("tmp_r",)], [wk_is])
            S_act(is_ap, is_ap, AF.Ln, [wk_is], [wk_is])
            S_act(is_ap, is_ap, AF.Exp, [wk_is], [wk_is], scale=-0.5)

        for cb in range(2):
            csl = slice(cb * 2048, (cb + 1) * 2048)
            for kt in range(KTD):
                kx = ("x0t",)
                DMA("Y", x0t, H["x0"][kt * 128:(kt + 1) * 128, csl], [], [kx], tag="A.x0")
                S_act(sq, x0t, AF.Square, [kx], [("sqA",)])
                for ch in range(4):
                    T_mm(PSA[0:1, ch * 512:(ch + 1) * 512], onesf[:, 0:1],
                         x0t[:, ch * 512:(ch + 1) * 512], kt == 0, kt == KTD - 1,
                         [kx, ("c", "onesf")], [bankA(ch)], tag="A.sum")
                    T_mm(PSB[0:1, ch * 512:(ch + 1) * 512], onesb[:, 0:1],
                         sq[:, ch * 512:(ch + 1) * 512], kt == 0, kt == KTD - 1,
                         [("sqA",), ("c", "ones")], [bankB(ch)], tag="A.sumsq")
            ln_rows(mu_r, is_r, tmp_r, PSA[0:1, 0:2048], PSB[0:1, 0:2048],
                    [bankA(0), bankA(1), bankA(2), bankA(3),
                     bankB(0), bankB(1), bankB(2), bankB(3)],
                    ("mu_rA",), ("is_rA",))
            for ch in range(4):
                T_mm(PSA[:, ch * 512:(ch + 1) * 512], onesf[0:1, :],
                     mu_r[0:1, ch * 512:(ch + 1) * 512],
                     True, True, [("c", "onesf"), ("mu_rA",)], [bankA(ch)], tag="A.bmu")
                T_mm(PSB[:, ch * 512:(ch + 1) * 512], onesf[0:1, :],
                     is_r[0:1, ch * 512:(ch + 1) * 512],
                     True, True, [("c", "onesf"), ("is_rA",)], [bankB(ch)], tag="A.bis")
            S_act(bmu, PSA[:, 0:2048], AF.Copy,
                  [bankA(0), bankA(1), bankA(2), bankA(3)], [("bmu",)])
            S_act(bis, PSB[:, 0:2048], AF.Copy,
                  [bankB(0), bankB(1), bankB(2), bankB(3)], [("bis",)])
            for kt in range(KTD):
                kx = ("x0t",)
                DMA("Y", x0t, H["x0"][kt * 128:(kt + 1) * 128, csl], [], [kx], tag="A.x0b")
                V_tt(ln1f, x0t, bmu, ALU.subtract, [kx, ("bmu",)], [("ln1f",)])
                V_tt(ln1f, ln1f, bis, ALU.mult, [("ln1f",), ("bis",)], [("ln1f",)], eng="P")
                DMA("Y", ln1_hbm[kt * 128:(kt + 1) * 128, csl], ln1f, [("ln1f",)],
                    [("ln1_hbm", kt, cb)], tag="A.ln1out")
                S_act(ln1b[:, kt, :], ln1f, AF.Copy, [("ln1f",)], [("ln1b", kt)])
            sc.link(("rnbrow4",), [("bmu",), ("bis",)])
            for kt in range(KTD):
                S_act(sq, ln1b[:, kt, :], AF.Square, [("ln1b", kt)], [("sqA",)])
                for ch in range(4):
                    T_mm(PSA[0:DN_HEADS, ch * 512:(ch + 1) * 512],
                         C["ind_dn"][:, kt * 4:(kt + 1) * 4],
                         sq[:, ch * 512:(ch + 1) * 512], kt == 0, kt == KTD - 1,
                         [("sqA",), ("c", "ind_dn")], [bankA(ch)], tag="A.rksum")
            V_ts(row4, PSA[0:DN_HEADS, 0:2048], float(np.float32(1e-24)), ALU.max,
                 [bankA(0), bankA(1), bankA(2), bankA(3)], [("row4k",)])
            S_act(row4, row4, AF.Ln, [("row4k",)], [("row4k",)])
            S_act(row4, row4, AF.Exp, [("row4k",)], [("row4k",)], scale=-0.5)
            for h in range(DN_HEADS):
                sel = C["sel_dn"][:, 2 * h * 128:(2 * h + 1) * 128]
                for ch in range(4):
                    T_mm(PSB[:, ch * 512:(ch + 1) * 512], sel,
                         row4[:, ch * 512:(ch + 1) * 512],
                         True, True, [("row4k",), ("c", "sel_dn")], [bankB(ch)],
                         tag="A.rnbc")
                S_act(rnb, PSB[:, 0:2048], AF.Copy,
                      [bankB(0), bankB(1), bankB(2), bankB(3)], [("rnbrow4",)])
                for j in range(2):
                    kt = 2 * h + j
                    V_tt(rk[:, kt, csl], ln1b[:, kt, :], rnb,
                         ALU.mult, [("ln1b", kt), ("rnbrow4",)],
                         [("rk", kt, cb)], eng="P")
            if "wwt" in dbg and cb == 0:
                for kt in range(KTD):
                    DMA("S", dbg["wwt"][kt * 128:(kt + 1) * 128, :], WwT[:, kt, :],
                        [("WwT", "all")], [("dbgwwt", kt)])
            sc.link(("vtb",), [("rnbrow4",)])
            for tt in range(16):
                t0 = tt * 128
                pb = 2 * (tt % 2)
                for k in range(KTD):
                    for fc in range(2):
                        T_mm(PSA[:, (pb + fc) * 512:(pb + fc + 1) * 512],
                             ln1b[:, k, t0:t0 + 128],
                             WwT[:, k, fc * 512:(fc + 1) * 512],
                             k == 0, k == KTD - 1,
                             [("ln1b", k), ("WwT", "all")], [bankA(pb + fc)],
                             tag="A.v")
                S_act(vtb, PSA[:, pb * 512:(pb + 2) * 512], AF.Copy,
                      [bankA(pb), bankA(pb + 1)], [("vtb",)])
                DMA("Y", v_hbm[cb * 2048 + t0:cb * 2048 + t0 + 128, :], vtb,
                    [("vtb",)], [("v_hbm", cb, tt)], tag="A.vout")
        ar.reset()

    if PHASE_LIMIT >= 1:
        emit_phaseA()
        sc.barrier()

    def emit_phaseB():
        ar.mark()
        dn_dbg = {}
        if os.environ.get("MK_DEBUG_DN"):
            for pr in (0, 2):
                dn_dbg[pr] = nc.dram_tensor(f"dn_dbg{pr}", [128, 1280], BF16,
                                            kind="ExternalOutput").ap()
        Sfp = [ar.alloc(f"Sfp{p}", [128, 2, DN_D], F32) for p in range(N_PAIR)]
        Sb = [ar.alloc(f"Sb{p}", [128, 2, DN_D], BF16) for p in range(N_PAIR)]
        for p in range(N_PAIR):
            memset(Sfp[p], 0.0, [], [("Sfp", p)])
            memset(Sb[p], 0.0, [], [("Sb", p)], eng="P")
        NSLOT = 4
        vch4 = [[ar.alloc(f"vch4_{s}_{j}", [64, DN_HEADS * DN_D], BF16)
                 for j in range(2)] for s in range(2)]
        w0t = [ar.alloc(f"w0t{i}", [128, 2, 64], BF16) for i in range(NSLOT)]
        Nb = [ar.alloc(f"Nb{i}", [64, 64], BF16) for i in range(NSLOT)]
        Mb = [ar.alloc(f"Mb{i}", [64, 64], BF16) for i in range(NSLOT)]
        Sacc = [[ar.alloc(f"Sacc{i}_{j}", [64, 64], BF16) for j in range(2)]
                for i in range(NSLOT)]
        Pb = [[ar.alloc(f"Pb{i}_{j}", [64, 128], BF16) for j in range(2)]
              for i in range(NSLOT)]
        ATb = [ar.alloc(f"ATb{i}", [64, 64], BF16) for i in range(NSLOT)]
        intraT = [ar.alloc(f"intraT{i}", [64, 64], BF16) for i in range(NSLOT)]
        wgpt = [ar.alloc(f"wgpt{i}", [64, 256], BF16) for i in range(NSLOT)]
        rkgT = [ar.alloc(f"rkgT{i}", [128, 2, 64], BF16) for i in range(NSLOT)]
        ub = [ar.alloc(f"ub{i}", [64, DN_D], BF16) for i in range(NSLOT)]
        vnb = [ar.alloc(f"vnb{i}", [64, DN_D], BF16) for i in range(NSLOT)]
        otile = [ar.alloc(f"otile{i}", [128, 2, 64], BF16) for i in range(NSLOT)]

        for n in range(DN_CHUNKS):
            for pr in range(N_PAIR):
                s_, h_ = divmod(pr, DN_HEADS)
                i = (n * N_PAIR + pr) % NSLOT
                P = (PSA, PSB)[i % 2]
                nm = ("PSA", "PSB")[i % 2]
                bb = 2 * ((i // 2) % 2)
                q0 = bb * 512
                q1 = (bb + 1) * 512
                k0, k1 = (nm, bb), (nm, bb + 1)
                c0 = s_ * T + n * 64
                kc = ("chunk", i)
                rkread = [("rk", 2 * h_, s_), ("rk", 2 * h_ + 1, s_)]
                if n == 0:
                    memset(w0t[i][:, :, 0:1], 0.0, [], [(kc, "w")])
                    V_copy(w0t[i][:, :, 1:64],
                           rk[:, 2 * h_:2 * h_ + 2, c0:c0 + 63],
                           rkread, [(kc, "w")], eng="P")
                    wsl = [w0t[i][:, j, :] for j in range(2)]
                    wread = [(kc, "w")]
                else:
                    wsl = [rk[:, 2 * h_ + j, c0 - 1:c0 + 63] for j in range(2)]
                    wread = rkread
                rsl = [rk[:, 2 * h_ + j, c0:c0 + 64] for j in range(2)]
                if h_ == 0:
                    DMA("S", vch4[s_][n % 2], v_hbm[c0:c0 + 64, :], [],
                        [("v4", s_, n % 2)], tag="B.v")
                T_mm(P[0:64, q0:q0 + 64], wsl[0], wsl[0], True, False, wread, [k0], "B.ww")
                T_mm(P[0:64, q0:q0 + 64], wsl[1], wsl[1], False, True, wread, [k0], "B.ww")
                T_mm(P[0:64, q0 + 64:q0 + 128], wsl[0], rsl[0], True, False,
                     wread + rkread, [k0], "B.wr")
                T_mm(P[0:64, q0 + 64:q0 + 128], wsl[1], rsl[1], False, True,
                     wread + rkread, [k0], "B.wr")
                V_tt(Nb[i], P[0:64, q0:q0 + 64], mlow_neg[h_], ALU.mult,
                     [k0, ("mlo", h_)], [(kc, "N")])
                V_tt(Mb[i], P[0:64, q0:q0 + 64], mup_neg[h_], ALU.mult,
                     [k0, ("mun", h_)], [(kc, "M")])
                V_tt(intraT[i], P[0:64, q0 + 64:q0 + 128], mup_pos[h_], ALU.mult,
                     [k0, ("mup", h_)], [(kc, "iT")])
                V_tt(Sacc[i][0], Nb[i], C["eye64b"][0:64, :], ALU.add,
                     [(kc, "N"), ("c", "eye64b")], [(kc, "S0")], eng="P")
                curS = 0
                nKey = mKey = None
                Nr, Mr = Nb[i], Mb[i]
                nKey, mKey = (kc, "N"), (kc, "M")
                for kit in range(1, 6):
                    if kit < 5:
                        T_mm(P[0:64, q1:q1 + 64], Mr, Nr, True, True,
                             [nKey, mKey], [k1], "B.Pn")
                    T_mm(P[0:64, q1 + 64:q1 + 128], Nr, Mr, True, True,
                         [nKey, mKey], [k1], "B.Ptn")
                    j = kit % 2
                    S_act(Pb[i][j], P[0:64, q1:q1 + 128], AF.Copy, [k1],
                          [(kc, "P", j)])
                    T_mm(P[0:64, q1 + 128:q1 + 192], Pb[i][j][:, 64:128], Sacc[i][curS],
                         True, True, [(kc, "P", j), (kc, f"S{curS}")], [k1],
                         "B.Sup")
                    newS = 1 - curS
                    V_tt(Sacc[i][newS], Sacc[i][curS], P[0:64, q1 + 128:q1 + 192],
                         ALU.add, [(kc, f"S{curS}"), k1], [(kc, f"S{newS}")])
                    curS = newS
                    Nr, Mr = Pb[i][j][:, 0:64], Pb[i][j][:, 64:128]
                    nKey = mKey = (kc, "P", j)
                pbT = P[:, q1 + 192:q1 + 224].bitcast(BF16)[0:64, 0:64]
                T_tr(pbT, Sacc[i][curS], eye[0:64, 0:64],
                     [(kc, f"S{curS}"), ("c", "eye")], [k1], "B.AT")
                S_act(ATb[i], pbT, AF.Copy, [k1], [(kc, "AT")])
                for j in range(2):
                    pbW = P[:, q1 + 224 + j * 64:q1 + 288 + j * 64].bitcast(BF16)[0:64, 0:128]
                    T_tr(pbW, wsl[j], eye, wread + [("c", "eye")], [k1], "B.wT")
                    S_act(wgpt[i][:, j * 128:(j + 1) * 128], pbW, AF.Copy,
                          [k1, ("gt", h_)], [(kc, "wg")], scale=gtail_col[h_])
                V_tt(rkgT[i], rk[:, 2 * h_:2 * h_ + 2, c0:c0 + 64],
                     gtile[h_].rearrange("p (a c) -> p a c", a=1).broadcast_to([128, 2, 64]),
                     ALU.mult, rkread + [("gtile", h_)], [(kc, "rg")], eng="P")
                T_mm(P[0:64, q0 + 256:q0 + 512], wsl[0], Sb[pr][:, 0, :], True, False,
                     wread + [("Sb", pr)], [k0], "B.wS")
                T_mm(P[0:64, q0 + 256:q0 + 512], wsl[1], Sb[pr][:, 1, :], False, True,
                     wread + [("Sb", pr)], [k0], "B.wS")
                V_stt(ub[i], P[0:64, q0 + 256:q0 + 512], -1.0,
                      vch4[s_][n % 2][:, h_ * DN_D:(h_ + 1) * DN_D], ALU.mult,
                      ALU.add, [k0, ("v4", s_, n % 2)], [(kc, "u")])
                T_mm(P[0:64, q0:q0 + 256], ATb[i], ub[i], True, True,
                     [(kc, "AT"), (kc, "u")], [k0], "B.Au")
                S_act(vnb[i], P[0:64, q0:q0 + 256], AF.Copy, [k0], [(kc, "vn")])
                for mv in range(2):
                    oc = slice(q1 + 352 + mv * 64, q1 + 352 + (mv + 1) * 64)
                    for kt in range(2):
                        T_mm(P[:, oc], Sb[pr][:, kt, mv * 128:(mv + 1) * 128],
                             rkgT[i][:, kt, :], kt == 0, False,
                             [("Sb", pr), (kc, "rg")], [k1], "B.o1")
                    T_mm(P[:, oc], vnb[i][:, mv * 128:(mv + 1) * 128], intraT[i],
                         False, True, [(kc, "vn"), (kc, "iT")], [k1], "B.o2")
                S_act(otile[i], P[:, q1 + 352:q1 + 480].rearrange("p (a c) -> p a c", a=2),
                      AF.Copy, [k1, ("al", h_)], [(kc, "ot")],
                      scale=al_col[h_])
                DMA("S", o_hbm[2 * h_ * 128:(2 * h_ + 2) * 128, c0:c0 + 64]
                    .rearrange("(a p) c -> p a c", p=128), otile[i],
                    [(kc, "ot")], [("o_hbm", pr, n)], tag="B.oout")
                for kt in range(2):
                    T_mm(P[:, q0 + kt * 256:q0 + (kt + 1) * 256],
                         wgpt[i][:, kt * 128:(kt + 1) * 128], vnb[i],
                         True, True, [(kc, "wg"), (kc, "vn")], [k0], "B.Sup2")
                V_stt(Sfp[pr].rearrange("p a c -> p (a c)"),
                      Sfp[pr].rearrange("p a c -> p (a c)"),
                      gC_col[h_], P[:, q0:q0 + 512], ALU.mult, ALU.add,
                      [("Sfp", pr), k0, ("gC", h_)], [("Sfp", pr)])
                S_act(Sb[pr], Sfp[pr], AF.Copy, [("Sfp", pr)], [("Sb", pr)])
        ar.reset()

    if PHASE_LIMIT >= 2:
        emit_phaseB()
        sc.barrier()

    def emit_phaseC():
        ar.mark()
        WrT = ar.alloc("WrT", [128, KTD, D], BF16)
        emit_wT(WrT, H["Wr"], "WrT")
        ocb = ar.alloc("ocb", [128, KTD, 2048], BF16)
        ln1t = ar.alloc("ln1tC", [128, 2048], F32)
        at = ar.alloc("atC", [128, 2048], F32)
        for cb in range(2):
            csl = slice(cb * 2048, (cb + 1) * 2048)
            for k in range(KTD):
                DMA("Y", ocb[:, k, :], o_hbm[k * 128:(k + 1) * 128, csl],
                    [], [("ocb", k)], tag="C.o")
            for m in range(KTD):
                DMA("Y", ln1t, ln1_hbm[m * 128:(m + 1) * 128, csl],
                    [("ln1_hbm", m, cb)], [("ln1tC",)], tag="C.ln1")
                for k in range(KTD):
                    for ch in range(4):
                        T_mm(PSA[:, ch * 512:(ch + 1) * 512],
                             WrT[:, k, m * 128:(m + 1) * 128],
                             ocb[:, k, ch * 512:(ch + 1) * 512],
                             k == 0, k == KTD - 1,
                             [("WrT", "all"), ("ocb", k)], [bankA(ch)], tag="C.proj")
                V_stt(at, PSA[:, 0:2048], 1.0, ln1t, ALU.mult, ALU.add,
                      [bankA(0), bankA(1), bankA(2), bankA(3), ("ln1tC",)],
                      [("atC",)])
                DMA("Y", a_hbm[m * 128:(m + 1) * 128, csl], at, [("atC",)],
                    [("a_hbm", m, cb)], tag="C.aout")
        ar.reset()

    if PHASE_LIMIT >= 3:
        emit_phaseC()
        sc.barrier()
    if "rk" in dbg:
        for kt in range(KTD):
            DMA("S", dbg["rk"][kt * 128:(kt + 1) * 128, :], rk[:, kt, :],
                [], [("dbgork", kt)])
    if "o" in dbg:
        for kt in range(KTD):
            DMA("S", dbg["o"][kt * 128:(kt + 1) * 128, :],
                o_hbm[kt * 128:(kt + 1) * 128, :], [], [("dbgoo", kt)])
    if "rk" in dbg or "o" in dbg:
        sc.barrier()
    ar.reset()

    def emit_phaseD():
        ar.mark()
        x0t = ar.alloc("x0tD", [128, 2048], F32)
        at = ar.alloc("atD", [128, 2048], F32)
        x1t = ar.alloc("x1tD", [128, 2048], F32)
        xb = ar.alloc("xbD", [128, 2048], BF16)
        sq = ar.alloc("sqD", [128, 2048], BF16)
        bmu = ar.alloc("bmuD", [128, 2048], F32)
        bis = ar.alloc("bisD", [128, 2048], F32)
        mu_r = ar.alloc("mu_rD", [1, 2048], F32)
        is_r = ar.alloc("is_rD", [1, 2048], F32)
        mu2_r = ar.alloc("mu2_rD", [1, 2048], F32)
        is2_r = ar.alloc("is2_rD", [1, 2048], F32)
        tmp_r = ar.alloc("tmp_rD", [1, 2048], F32)

        def rowstats(mu_ap, is_ap, wk_mu, wk_is, keys):
            V_ts(mu_ap, PSA[0:1, 0:2048], 1.0 / D, ALU.mult, keys, [wk_mu])
            V_ts(is_ap, PSA[32:33, 0:2048], 1.0 / D, ALU.mult, keys, [wk_is],
                 s2=LNEPS, op1=ALU.add)
            V_tt(tmp_r, mu_ap, mu_ap, ALU.mult, [wk_mu], [("tmp_rD",)])
            V_tt(is_ap, is_ap, tmp_r, ALU.subtract, [wk_is, ("tmp_rD",)], [wk_is])
            S_act(is_ap, is_ap, AF.Ln, [wk_is], [wk_is])
            S_act(is_ap, is_ap, AF.Exp, [wk_is], [wk_is], scale=-0.5)

        def statpass(src_hbm, src_key, wk_mu, wk_is, mu_ap, is_ap, cb, tag):
            csl = slice(cb * 2048, (cb + 1) * 2048)
            for m in range(KTD):
                DMA("Y", x1t, src_hbm[m * 128:(m + 1) * 128, csl],
                    [(src_key, m, cb)], [("x1tD",)], tag=tag)
                S_act(xb, x1t, AF.Copy, [("x1tD",)], [("xbD",)])
                S_act(sq, xb, AF.Square, [("xbD",)], [("sqD",)])
                for ch in range(4):
                    T_mm(PSA[0:1, ch * 512:(ch + 1) * 512], onesb[:, 0:1],
                         xb[:, ch * 512:(ch + 1) * 512], m == 0, False,
                         [("xbD",), ("c", "ones")], [bankA(ch)], tag=tag + "s",
                         sgc=True)
                    T_mm(PSA[32:33, ch * 512:(ch + 1) * 512], onesb[:, 0:1],
                         sq[:, ch * 512:(ch + 1) * 512], False, m == KTD - 1,
                         [("sqD",), ("c", "ones")], [bankA(ch)], tag=tag + "q",
                         sgc=True)
            rowstats(mu_ap, is_ap, wk_mu, wk_is,
                     [bankA(0), bankA(1), bankA(2), bankA(3)])

        def bcast(mu_ap, is_ap, rk_mu, rk_is):
            for ch in range(4):
                T_mm(PSA[:, ch * 512:(ch + 1) * 512], onesf[0:1, :],
                     mu_ap[0:1, ch * 512:(ch + 1) * 512], True, True,
                     [("c", "onesf"), rk_mu], [bankA(ch)], tag="D.bmu")
                T_mm(PSB[:, ch * 512:(ch + 1) * 512], onesf[0:1, :],
                     is_ap[0:1, ch * 512:(ch + 1) * 512], True, True,
                     [("c", "onesf"), rk_is], [bankB(ch)], tag="D.bis")
            S_act(bmu, PSA[:, 0:2048], AF.Copy,
                  [bankA(0), bankA(1), bankA(2), bankA(3)], [("bmuD",)])
            S_act(bis, PSB[:, 0:2048], AF.Copy,
                  [bankB(0), bankB(1), bankB(2), bankB(3)], [("bisD",)])

        for cb in range(2):
            csl = slice(cb * 2048, (cb + 1) * 2048)
            statpass(a_hbm, "a_hbm", ("muD",), ("isD",), mu_r, is_r, cb, "D.a1")
            bcast(mu_r, is_r, ("muD",), ("isD",))
            for m in range(KTD):
                DMA("Y", x0t, H["x0"][m * 128:(m + 1) * 128, csl], [],
                    [("x0tD",)], tag="D.x0")
                DMA("Y", at, a_hbm[m * 128:(m + 1) * 128, csl],
                    [("a_hbm", m, cb)], [("atD",)], tag="D.a2")
                V_tt(at, at, bmu, ALU.subtract, [("atD",), ("bmuD",)], [("atD",)])
                V_tt(at, at, bis, ALU.mult, [("atD",), ("bisD",)], [("atD",)], eng="P")
                V_tt(x1t, x0t, at, ALU.add, [("x0tD",), ("atD",)], [("x1tD",)])
                DMA("Y", x1_hbm[m * 128:(m + 1) * 128, csl], x1t, [("x1tD",)],
                    [("x1_hbm", m, cb)], tag="D.x1out")
                S_act(xb, x1t, AF.Copy, [("x1tD",)], [("xbD",)])
                S_act(sq, xb, AF.Square, [("xbD",)], [("sqD",)])
                for ch in range(4):
                    T_mm(PSA[0:1, ch * 512:(ch + 1) * 512], onesb[:, 0:1],
                         xb[:, ch * 512:(ch + 1) * 512], m == 0, False,
                         [("xbD",), ("c", "ones")], [bankA(ch)], tag="D.s2",
                         sgc=True)
                    T_mm(PSA[32:33, ch * 512:(ch + 1) * 512], onesb[:, 0:1],
                         sq[:, ch * 512:(ch + 1) * 512], False, m == KTD - 1,
                         [("sqD",), ("c", "ones")], [bankA(ch)], tag="D.q2",
                         sgc=True)
            rowstats(mu2_r, is2_r, ("mu2D",), ("is2D",),
                     [bankA(0), bankA(1), bankA(2), bankA(3)])
            bcast(mu2_r, is2_r, ("mu2D",), ("is2D",))
            for m in range(KTD):
                DMA("Y", x1t, x1_hbm[m * 128:(m + 1) * 128, csl],
                    [("x1_hbm", m, cb)], [("x1tD",)], tag="D.x1in")
                V_tt(at, x1t, bmu, ALU.subtract, [("x1tD",), ("bmuD",)], [("atD",)])
                V_tt(at, at, bis, ALU.mult, [("atD",), ("bisD",)], [("atD",)], eng="P")
                DMA("Y", xin_hbm[m * 128:(m + 1) * 128, csl], at, [("atD",)],
                    [("xin_hbm", m, cb)], tag="D.xinout")
        ar.reset()

    if PHASE_LIMIT >= 4:
        emit_phaseD()
        sc.barrier()

    emit_inner = build_inner(nc, sc, es, ar, H, CK, C, dbg, locals())

    for l in range(IN_DEPTH):
        if PHASE_LIMIT >= 5 + l:
            emit_inner(l)
            sc.barrier()

    def emit_phaseG():
        ar.mark()
        ft = ar.alloc("ftG", [128, 2048], F32)
        x1t = ar.alloc("x1tG", [128, 2048], F32)
        xb = ar.alloc("xbG", [128, 2048], BF16)
        sq = ar.alloc("sqG", [128, 2048], BF16)
        bmu = ar.alloc("bmuG", [128, 2048], F32)
        bis = ar.alloc("bisG", [128, 2048], F32)
        mu_r = ar.alloc("mu_rG", [1, 2048], F32)
        is_r = ar.alloc("is_rG", [1, 2048], F32)
        tmp_r = ar.alloc("tmp_rG", [1, 2048], F32)
        for cb in range(2):
            csl = slice(cb * 2048, (cb + 1) * 2048)
            for m in range(KTD):
                DMA("Y", ft, xin_hbm[m * 128:(m + 1) * 128, csl],
                    [("xin_hbm", m, cb)], [("ftG",)], tag="G.f")
                S_act(xb, ft, AF.Copy, [("ftG",)], [("xbG",)])
                S_act(sq, xb, AF.Square, [("xbG",)], [("sqG",)])
                for ch in range(4):
                    T_mm(PSA[0:1, ch * 512:(ch + 1) * 512], onesb[:, 0:1],
                         xb[:, ch * 512:(ch + 1) * 512], m == 0, False,
                         [("xbG",), ("c", "ones")], [bankA(ch)], tag="G.sum",
                         sgc=True)
                    T_mm(PSA[32:33, ch * 512:(ch + 1) * 512], onesb[:, 0:1],
                         sq[:, ch * 512:(ch + 1) * 512], False, m == KTD - 1,
                         [("sqG",), ("c", "ones")], [bankA(ch)], tag="G.sumsq",
                         sgc=True)
            V_ts(mu_r, PSA[0:1, 0:2048], 1.0 / D, ALU.mult,
                 [bankA(0), bankA(1), bankA(2), bankA(3)], [("muG",)])
            V_ts(is_r, PSA[32:33, 0:2048], 1.0 / D, ALU.mult,
                 [bankA(0), bankA(1), bankA(2), bankA(3)], [("isG",)],
                 s2=LNEPS, op1=ALU.add)
            V_tt(tmp_r, mu_r, mu_r, ALU.mult, [("muG",)], [("tmpG",)])
            V_tt(is_r, is_r, tmp_r, ALU.subtract, [("isG",), ("tmpG",)], [("isG",)])
            S_act(is_r, is_r, AF.Ln, [("isG",)], [("isG",)])
            S_act(is_r, is_r, AF.Exp, [("isG",)], [("isG",)], scale=-0.5)
            for ch in range(4):
                T_mm(PSA[:, ch * 512:(ch + 1) * 512], onesf[0:1, :],
                     mu_r[0:1, ch * 512:(ch + 1) * 512], True, True,
                     [("c", "onesf"), ("muG",)], [bankA(ch)], tag="G.bmu")
                T_mm(PSB[:, ch * 512:(ch + 1) * 512], onesf[0:1, :],
                     is_r[0:1, ch * 512:(ch + 1) * 512], True, True,
                     [("c", "onesf"), ("isG",)], [bankB(ch)], tag="G.bis")
            S_act(bmu, PSA[:, 0:2048], AF.Copy,
                  [bankA(0), bankA(1), bankA(2), bankA(3)], [("bmuG",)])
            S_act(bis, PSB[:, 0:2048], AF.Copy,
                  [bankB(0), bankB(1), bankB(2), bankB(3)], [("bisG",)])
            for m in range(KTD):
                DMA("Y", ft, xin_hbm[m * 128:(m + 1) * 128, csl],
                    [("xin_hbm", m, cb)], [("ftG",)], tag="G.f2")
                DMA("Y", x1t, x1_hbm[m * 128:(m + 1) * 128, csl],
                    [("x1_hbm", m, cb)], [("x1tG",)], tag="G.x1")
                V_tt(ft, ft, bmu, ALU.subtract, [("ftG",), ("bmuG",)], [("ftG",)])
                V_tt(ft, ft, bis, ALU.mult, [("ftG",), ("bisG",)], [("ftG",)], eng="P")
                V_tt(ft, ft, x1t, ALU.add, [("ftG",), ("x1tG",)], [("ftG",)])
                DMA("Y", out_h[m * 128:(m + 1) * 128, csl], ft, [("ftG",)],
                    [("out", m, cb)], tag="G.out")
        ar.reset()

    if PHASE_LIMIT >= 7:
        emit_phaseG()
    else:
        sc.barrier()
        zt = ar.alloc("zout", [128, 2048], F32)
        memset(zt, 0.0, [], [("zout",)])
        for m in range(KTD):
            for cb in range(2):
                DMA("Y", out_h[m * 128:(m + 1) * 128, cb * 2048:(cb + 1) * 2048],
                    zt, [("zout",)], [("out", m, cb)], tag="zero.out")

    def dump(name, src_hbm):
        if name in dbg:
            tt = ar.alloc("dbgt" + name, [128, 2048], dbg[name].tensor.dtype)
            rows = dbg[name].shape[0]
            for m in range(rows // 128):
                for cb in range(2):
                    DMA("S", tt, src_hbm[m * 128:(m + 1) * 128,
                                         cb * 2048:(cb + 1) * 2048],
                        [], [("dbgt", name)], tag="dbg")
                    DMA("S", dbg[name][m * 128:(m + 1) * 128,
                                       cb * 2048:(cb + 1) * 2048],
                        tt, [("dbgt", name)], [("dbgo", name, m, cb)], tag="dbg")

    if dbg:
        sc.barrier()
        dump("ln1", ln1_hbm)
        dump("a", a_hbm)
        dump("x1", x1_hbm)
        dump("xin", xin_hbm)
        if "v" in dbg:
            tt = ar.alloc("dbgtv", [128, 1024], BF16)
            for m in range(M // 128):
                DMA("S", tt, v_hbm[m * 128:(m + 1) * 128, :], [], [("dbgtv",)])
                DMA("S", dbg["v"][m * 128:(m + 1) * 128, :], tt, [("dbgtv",)],
                    [("dbgov", m)])

    sc.materialize(nc, es)
    return nc, es


def build_inner(nc, sc, es, ar, H, CK, C, dbg, O):
    T_mm, T_tr, S_act = O["T_mm"], O["T_tr"], O["S_act"]
    V_tt, V_stt, V_ts = O["V_tt"], O["V_stt"], O["V_ts"]
    DMA, V_copy, V_red, memset = O["DMA"], O["V_copy"], O["V_red"], O["memset"]
    PSA, PSB = O["PSA"], O["PSB"]
    bankA, bankB = O["bankA"], O["bankB"]
    eye, onesb, onesf = O["eye"], O["onesb"], O["onesf"]
    xin_hbm = O["xin_hbm"]
    CK = O["CK"]
    wqkv_bf, wo_bf, w1_bf, w2_bf = O["wqkv_bf"], O["wo_bf"], O["w1_bf"], O["w2_bf"]


    PREVKEYS = [[], []]

    def call_fence(mykeys):
        allk = PREVKEYS[0] + PREVKEYS[1]
        for k in mykeys:
            sc.link(k, allk)
        PREVKEYS[1] = PREVKEYS[0]
        PREVKEYS[0] = list(mykeys)

    def emit_inner(l):
        ar.mark()
        lk = ("L", l)
        qs_c = ar.alloc("qs_c", [128, 2], F32)
        ks_c = ar.alloc("ks_c", [128, 2], F32)
        for (nm, hsrc, dst) in (("qs", H["q_scale"], qs_c), ("ks", H["k_scale"], ks_c)):
            DMA("Y", dst[0:64, 0:1], hsrc[l:l + 1, :], [], [(lk, nm, 0)], tag="I.qs")
            V_copy(dst[0:32, 1:2], dst[32:64, 0:1], [(lk, nm, 0)], [(lk, nm, 1)])
            V_copy(dst[32:64, 1:2], dst[0:32, 0:1], [(lk, nm, 0)], [(lk, nm, 1)])
            V_copy(dst[64:128, 0:2], dst[0:64, 0:2],
                   [(lk, nm, 0), (lk, nm, 1)], [(lk, nm, 2)])
        lg_c = ar.alloc("lg_c", [128, KTD], F32)
        lb_c = ar.alloc("lb_c", [128, KTD], F32)
        b2_c = ar.alloc("b2_c", [128, KTD], F32)
        for kt in range(KTD):
            DMA("Y", lg_c[:, kt:kt + 1], H["ln_g"][l:l + 1, kt * 128:(kt + 1) * 128],
                [], [(lk, "lg")], tag="I.lg")
            DMA("Y", lb_c[:, kt:kt + 1], H["ln_b"][l:l + 1, kt * 128:(kt + 1) * 128],
                [], [(lk, "lb")], tag="I.lb")
            DMA("Y", b2_c[:, kt:kt + 1], H["b2"][l:l + 1, kt * 128:(kt + 1) * 128],
                [], [(lk, "b2")], tag="I.b2")
        b1a_c = ar.alloc("b1a_c", [128, 22], F32)
        b1g_c = ar.alloc("b1g_c", [128, 22], F32)
        for mi in range(22):
            n = min(128, HID - mi * 128)
            DMA("Y", b1a_c[0:n, mi:mi + 1], H["b1"][l:l + 1, mi * 128:mi * 128 + n],
                [], [(lk, "b1")], tag="I.b1")
            DMA("Y", b1g_c[0:n, mi:mi + 1],
                H["b1"][l:l + 1, HID + mi * 128:HID + mi * 128 + n],
                [], [(lk, "b1")], tag="I.b1")
        for s in range(SEQ):
            emit_attention(l, s, lk, qs_c, ks_c)
            if l == 0 and "xwo" in dbg:
                for kt in range(KTD):
                    DMA("S", dbg["xwo"][kt * 128:(kt + 1) * 128, s * T:(s + 1) * T],
                        xin_hbm[kt * 128:(kt + 1) * 128, s * T:(s + 1) * T],
                        [("xin_hbm", kt, s)], [("dbgxwo", kt, s)])
            emit_chain(l, s, lk, lg_c, lb_c, b1a_c, b1g_c, b2_c)
        ar.reset()

    def emit_attention(l, s, lk, qs_c, ks_c):
        ar.mark()
        sk = ("A", l, s)
        col0 = s * T
        call_fence(
            [(sk, "hb", kt) for kt in range(KTD)]
            + [(sk, "oT"), (sk, "vT"), (sk, "qx"), (sk, "kx"), ("xtI",),
               (sk, "scr"), (sk, "swb"), (sk, "nrm", "q"), (sk, "nrm", "k"),
               (sk, "kmd"), (sk, "Rq")]
            + [(sk, "E", r) for r in range(8)]
            + [(sk, nm, k) for nm in ("qraw", "kraw", "vbuf") for k in range(4)]
            + [(sk, nm, p) for nm in ("ot1", "ot2", "rr") for p in range(2)]
            + [(sk, "TB", nm) for nm in ("cq", "sq", "ck", "sk")]
            + [(sk, nm, g) for nm in ("wq", "wot") for g in range(2)])
        hb = ar.alloc("hb", [128, KTD, T], BF16)
        oT = ar.alloc_at("oT", [128, 4, T], BF16, ar.offsets["hb"])
        Esb = ar.alloc_at("Esb", [128, 8, 512], BF16, ar.offsets["hb"] + 16384)
        qext = ar.alloc_at("qext", [65, T], BF16, ar.offsets["hb"] + 16384 + 8192)
        kext = ar.alloc_at("kext", [65, T], BF16, ar.offsets["hb"] + 16384 + 8192 + 4096)
        vT = ar.alloc("vTI", [128, 16, 130], BF16)
        qraw = ar.alloc("qraw", [128, 4, T], BF16)
        kraw = ar.alloc("kraw", [128, 4, T], BF16)
        vbuf = ar.alloc("vbuf", [128, 4, T], BF16)
        xt = ar.alloc("xtI", [128, T], F32)
        scr = ar.alloc("scrI", [128, T], BF16)
        swb = ar.alloc("swbI", [128, T], BF16)
        nrm_q = ar.alloc("nrmq", [8, T], F32)
        nrm_k = ar.alloc("nrmk", [8, T], F32)
        kmdw = ar.alloc("kmdw", [128, 4, 4], F32)
        Rq = ar.alloc("Rq", [8, T], BF16)
        otmp = [ar.alloc(f"otmpI{i}", [64, 512], BF16) for i in range(2)]
        otmp2 = [ar.alloc(f"otmpI2{i}", [64, 512], BF16) for i in range(2)]
        rrow = [ar.alloc(f"rrowI{i}", [128, 512], F32) for i in range(2)]
        TB = {}
        for nm in ("cq", "sq", "ck", "sk"):
            TB[nm] = ar.alloc("TB" + nm, [128, T], BF16)
        wqkv_t = [ar.alloc(f"wqkv_t{i}", [128, KTD, 128], BF16) for i in range(2)]
        wo_t = [ar.alloc(f"wo_t{i}", [128, 4, 128], BF16) for i in range(2)]
        for nm, base, scol, sidx in (("cq", "cs_q", qs_c, 0), ("sq", "ss_q", qs_c, 1),
                                     ("ck", "cs_k", ks_c, 0), ("sk", "ss_k", ks_c, 1)):
            DMA("Y", xt, CK[base], [], [("xtI",)], tag="I.tb")
            S_act(TB[nm], xt, AF.Copy,
                  [("xtI",), (lk, "qs", 2), (lk, "ks", 2)], [(sk, "TB", nm)],
                  scale=scol[:, sidx:sidx + 1])
        for kt in range(KTD):
            kx = ("xtI",)
            DMA("Y", xt, xin_hbm[kt * 128:(kt + 1) * 128, col0:col0 + T],
                [("xin_hbm", kt, s)], [kx], tag="I.h.x")
            if kt < 4:
                S_act(hb[:, kt, :], xt, AF.Copy, [kx], [(sk, "hb", kt)])
            else:
                memset(hb[:, kt, 0:1], 0.0, [], [(sk, "hb", kt)])
                S_act(hb[:, kt, 1:T], xt[:, 0:T - 1], AF.Copy,
                      [kx], [(sk, "hb", kt)])
        for mt in range(12):
            gi = mt % 2
            tb = (l * 12 + mt) * 128
            DMA("S", wqkv_t[gi][:],
                wqkv_bf[tb:tb + 128, :].rearrange("p (a c) -> p a c", a=KTD),
                [("wqi", l, mt)], [(sk, "wq", gi)], tag="I.wqkv")
            P, nm = (PSA, "PSA") if mt % 2 == 0 else (PSB, "PSB")
            for kt in range(KTD):
                for ch in range(4):
                    T_mm(P[:, ch * 512:(ch + 1) * 512],
                         wqkv_t[gi][:, kt, :],
                         hb[:, kt, ch * 512:(ch + 1) * 512],
                         kt == 0, kt == KTD - 1,
                         [(sk, "wq", gi), (sk, "hb", kt)], [(nm, ch)], tag="I.qkv")
            if mt < 4:
                dst, dk = qraw[:, mt, :], (sk, "qraw", mt)
            elif mt < 8:
                dst, dk = kraw[:, mt - 4, :], (sk, "kraw", mt - 4)
            else:
                dst, dk = vbuf[:, mt - 8, :], (sk, "vbuf", mt - 8)
            S_act(dst, P[:, 0:T], AF.Copy,
                  [(nm, 0), (nm, 1), (nm, 2), (nm, 3)], [dk], tag="I.qkv.ev")
        for gname, raw, tcos, tsin in (("q", qraw, TB["cq"], TB["sq"]),
                                       ("k", kraw, TB["ck"], TB["sk"])):
            rkey = "qraw" if gname == "q" else "kraw"
            for kt in range(4):
                S_act(scr, raw[:, kt, :], AF.Square, [(sk, rkey, kt)], [(sk, "scr")])
                for ch in range(4):
                    T_mm(PSA[0:8, ch * 512:(ch + 1) * 512],
                         C["ind_in"][:, kt * 8:(kt + 1) * 8],
                         scr[:, ch * 512:(ch + 1) * 512], kt == 0, kt == 3,
                         [(sk, "scr"), ("c", "ind_in")], [bankA(ch)], tag="I.ss")
            nrm = nrm_q if gname == "q" else nrm_k
            nkey = (sk, "nrm", gname)
            V_ts(nrm, PSA[0:8, 0:T], float(np.float32(1e-24)), ALU.max,
                 [bankA(0), bankA(1), bankA(2), bankA(3)], [nkey])
            S_act(nrm, nrm, AF.Ln, [nkey], [nkey])
            S_act(nrm, nrm, AF.Exp, [nkey], [nkey], scale=-0.5)
            for kt in range(4):
                sel = C["sel_in"][:, kt * 128:(kt + 1) * 128]
                for ch in range(4):
                    T_mm(PSB[:, ch * 512:(ch + 1) * 512], sel,
                         nrm[:, ch * 512:(ch + 1) * 512], True, True,
                         [nkey, ("c", "sel_in")], [bankB(ch)], tag="I.rn")
                V_tt(raw[:, kt, :], raw[:, kt, :], PSB[:, 0:T], ALU.mult,
                     [(sk, rkey, kt), bankB(0), bankB(1), bankB(2), bankB(3)],
                     [(sk, rkey, kt)], tag="I.nrm")
                for ch in range(4):
                    T_mm(PSA[:, ch * 512:(ch + 1) * 512], C["Pswap"],
                         raw[:, kt, ch * 512:(ch + 1) * 512], True, True,
                         [(sk, rkey, kt), ("c", "Pswap")], [bankA(ch)], tag="I.sw")
                V_tt(scr, raw[:, kt, :], tcos, ALU.mult,
                     [(sk, rkey, kt), (sk, "TB", "c" + gname)], [(sk, "scr")],
                     eng="P")
                S_act(swb, PSA[:, 0:T], AF.Copy,
                      [bankA(0), bankA(1), bankA(2), bankA(3)], [(sk, "swb")])
                V_tt(swb, swb, tsin, ALU.mult,
                     [(sk, "swb"), (sk, "TB", "s" + gname)], [(sk, "swb")], eng="P")
                V_tt(raw[:, kt, :], scr, swb, ALU.add,
                     [(sk, "scr"), (sk, "swb")], [(sk, rkey, kt)], tag="I.rot")
        if l == 0 and s == 0 and "qf" in dbg:
            for kt in range(4):
                DMA("S", dbg["qf"][kt * 128:(kt + 1) * 128, 0:T], qraw[:, kt, :],
                    [(sk, "qraw", kt)], [("dbgqf", kt)])
                DMA("S", dbg["kf"][kt * 128:(kt + 1) * 128, 0:T], kraw[:, kt, :],
                    [(sk, "kraw", kt)], [("dbgkf", kt)])
                DMA("S", dbg["vv"][kt * 128:(kt + 1) * 128, 0:T], vbuf[:, kt, :],
                    [(sk, "vbuf", kt)], [("dbgvv", kt)])
        for kt in range(4):
            for w in range(NW):
                V_red(kmdw[:, kt, w:w + 1],
                      kraw[:, kt, w * WINDOW:(w + 1) * WINDOW], ALU.max,
                      [(sk, "kraw", kt)], [(sk, "kmd")], absval=True, tag="I.kmd")
        for kt in range(4):
            for w in range(NW - 1, 0, -1):
                V_tt(kmdw[:, kt, w:w + 1], kmdw[:, kt, w:w + 1],
                     kmdw[:, kt, w - 1:w], ALU.max, [(sk, "kmd")], [(sk, "kmd")],
                     tag="I.kmd2")
        for kt in range(4):
            V_stt(scr, qraw[:, kt, :], -1.0, qraw[:, kt, :], ALU.mult, ALU.max,
                  [(sk, "qraw", kt)], [(sk, "scr")], tag="I.absq")
            for w in range(NW):
                V_ts(swb[:, w * WINDOW:(w + 1) * WINDOW],
                     scr[:, w * WINDOW:(w + 1) * WINDOW], kmdw[:, kt, w:w + 1],
                     ALU.mult, [(sk, "scr"), (sk, "kmd")], [(sk, "swb")],
                     tag="I.bterm")
            for ch in range(4):
                T_mm(PSA[0:8, ch * 512:(ch + 1) * 512],
                     C["ind_in"][:, kt * 8:(kt + 1) * 8],
                     swb[:, ch * 512:(ch + 1) * 512], kt == 0, kt == 3,
                     [(sk, "swb"), ("c", "ind_in")], [bankA(ch)], tag="I.bsum")
        V_ts(Rq, PSA[0:8, 0:T], -1.0, ALU.mult,
             [bankA(0), bankA(1), bankA(2), bankA(3)], [(sk, "Rq")], tag="I.Rq")
        sc.link((sk, "oT"), [(sk, "hb", kt) for kt in range(KTD)])
        for r in range(8):
            sc.link((sk, "E", r), [(sk, "hb", kt) for kt in range(KTD)])
        sc.link((sk, "qx"), [(sk, "hb", kt) for kt in range(KTD)])
        sc.link((sk, "kx"), [(sk, "hb", kt) for kt in range(KTD)])
        EPS30 = float(np.float32(1e-30))
        for h in range(IN_HEADS):
            kt4, r0 = h // 2, (h % 2) * 64
            V_copy(qext[0:64, :], qraw[r0:r0 + 64, kt4, :],
                   [(sk, "qraw", kt4)], [(sk, "qx")], tag="I.qx")
            DMA("S", qext[64:65, :], Rq[h:h + 1, :], [(sk, "Rq")], [(sk, "qx")],
                tag="I.qxB")
            V_copy(kext[0:64, :], kraw[r0:r0 + 64, kt4, :],
                   [(sk, "kraw", kt4)], [(sk, "kx")], tag="I.kx")
            memset(kext[64:65, :], 1.0, [], [(sk, "kx")])
            if h % 2 == 0:
                for tt in range(16):
                    pbt = PSA[:, 1024 + (tt % 2) * 64:1024 + (tt % 2) * 64 + 64]\
                        .bitcast(BF16)[:, 0:128]
                    T_tr(pbt, vbuf[:, kt4, tt * 128:(tt + 1) * 128], eye,
                         [(sk, "vbuf", kt4), ("c", "eye")], [bankA(2)],
                         tag="I.vT")
                    S_act(vT[:, tt, 0:64], pbt[:, 0:64], AF.Copy, [bankA(2)],
                          [(sk, "vT")], tag="I.vTev")
                    S_act(vT[:, tt, 65:129], pbt[:, 64:128], AF.Copy, [bankA(2)],
                          [(sk, "vT")], tag="I.vTev2")
                memset(vT[:, :, 64:65], 1.0, [], [(sk, "vT")])
                memset(vT[:, :, 129:130], 1.0, [], [(sk, "vT")], eng="P")
            vsl = slice(r0 + (h % 2), r0 + (h % 2) + 65)
            for w in range(NW):
                if w == 0:
                    stripes = {r: (r, r, 3, ("maskB", r)) for r in range(4)}
                    avorder = [0, 1, 2, 3]
                else:
                    stripes = {}
                    for r in range(8):
                        mk = ("maskA", r) if r <= 3 else ("maskB", r - 4)
                        stripes[r] = (4 * (w - 1) + r, max(0, r - 4),
                                      min(3, r), mk)
                    avorder = [3, 4, 0, 1, 2, 5, 6, 7]
                for r, (kt_abs, lo, hi, (mknm, mkqt)) in stripes.items():
                    width = (hi - lo + 1) * 128
                    slot = r % 4
                    P2, nm2 = (PSA, "PSA") if slot < 2 else (PSB, "PSB")
                    bk = slot % 2
                    ps = P2[:, bk * 512:bk * 512 + width]
                    T_mm(ps, kext[:, kt_abs * 128:(kt_abs + 1) * 128],
                         qext[:, (4 * w + lo) * 128:(4 * w + hi + 1) * 128],
                         True, True, [(sk, "qx"), (sk, "kx")], [(nm2, bk)],
                         tag="I.sim")
                    mo = (mkqt - lo) * 128
                    V_tt(P2[:, bk * 512 + mo:bk * 512 + mo + 128],
                         P2[:, bk * 512 + mo:bk * 512 + mo + 128],
                         C[mknm], ALU.add,
                         [(nm2, bk), ("c", mknm)], [(nm2, bk)], tag="I.msk")
                    S_act(Esb[:, r, lo * 128:lo * 128 + width], ps, AF.Exp,
                          [(nm2, bk)], [(sk, "E", r)], tag="I.exp")
                wp = w % 2
                Ppo, pnm = (PSB, "PSB") if wp else (PSA, "PSA")
                pok = (pnm, 3)
                po = Ppo[0:65, 1536:2048]
                rbank = Ppo[0:64, 1024:1536]
                rbk = (pnm, 2)
                for j, r in enumerate(avorder):
                    kt_abs, lo, hi, _ = stripes[r]
                    T_mm(po[:, lo * 128:(hi + 1) * 128], vT[:, kt_abs, vsl],
                         Esb[:, r, lo * 128:(hi + 1) * 128],
                         j == 0, j == len(avorder) - 1,
                         [(sk, "vT"), (sk, "E", r)], [pok], tag="I.av")
                V_ts(rrow[wp][64:65, :], po[64:65, :], EPS30,
                     ALU.add, [pok], [(sk, "rr", wp)], tag="I.rsum")
                sc.emit("V", (lambda E, o=rrow[wp][64:65, :],
                              i=rrow[wp][64:65, :]: E.reciprocal(o, i)),
                        reads=[(sk, "rr", wp)], writes=[(sk, "rr", wp)],
                        tag="I.rcp")
                T_mm(rbank, onesf[64:65, 0:64], rrow[wp][64:65, :], True, True,
                     [(sk, "rr", wp), ("c", "onesf")], [rbk], tag="I.rbc")
                S_act(otmp[wp], po[0:64, :], AF.Copy, [pok], [(sk, "ot1", wp)],
                      tag="I.oev")
                V_tt(otmp2[wp], otmp[wp], rbank, ALU.mult,
                     [(sk, "ot1", wp), rbk], [(sk, "ot2", wp)], tag="I.onrm")
                V_copy(oT[r0:r0 + 64, kt4, w * 512:(w + 1) * 512],
                       otmp2[wp], [(sk, "ot2", wp)], [(sk, "oT")], tag="I.ocp")
        if l == 0 and s == 0 and "ot" in dbg:
            for kt in range(4):
                DMA("S", dbg["ot"][kt * 128:(kt + 1) * 128, 0:T], oT[:, kt, :],
                    [(sk, "oT")], [("dbgot", kt)])
        for mt in range(KTD):
            gi = mt % 2
            tb = (l * KTD + mt) * 128
            DMA("S", wo_t[gi][:],
                wo_bf[tb:tb + 128, :].rearrange("p (a c) -> p a c", a=4),
                [("woi", l, mt)], [(sk, "wot", gi)], tag="I.wo.w")
            P, nm = (PSA, "PSA") if mt % 2 == 0 else (PSB, "PSB")
            for kt in range(4):
                for ch in range(4):
                    T_mm(P[:, ch * 512:(ch + 1) * 512],
                         wo_t[gi][:, kt, :],
                         oT[:, kt, ch * 512:(ch + 1) * 512],
                         kt == 0, kt == 3,
                         [(sk, "wot", gi), (sk, "oT")], [(nm, ch)], tag="I.wo")
            kx = ("xtI",)
            DMA("Y", xt, xin_hbm[mt * 128:(mt + 1) * 128, col0:col0 + T],
                [("xin_hbm", mt, s)], [kx], tag="I.wo.x")
            V_stt(xt, P[:, 0:T], 1.0, xt, ALU.mult, ALU.add,
                  [(nm, 0), (nm, 1), (nm, 2), (nm, 3), kx], [kx], tag="I.wo.add")
            DMA("Y", xin_hbm[mt * 128:(mt + 1) * 128, col0:col0 + T], xt,
                [kx], [("xin_hbm", mt, s)], tag="I.wo.out")
        ar.reset()

    def emit_chain(l, s, lk, lg_c, lb_c, b1a_c, b1g_c, b2_c):
        ar.mark()
        sk = ("F", l, s)
        col0 = s * T
        call_fence(
            [("xtF",), (sk, "sq"), (sk, "bmu"), (sk, "bis"),
             (sk, "mur"), (sk, "isr"), (sk, "tmpr")]
            + [(sk, "shb", kt) for kt in range(KTD)]
            + [(sk, "h2", kt) for kt in range(KTD)]
            + [(sk, "h3", mi) for mi in range(22)]
            + [(sk, nm, g) for nm in ("w1a", "w1g", "w2c", "gt")
               for g in range(2)])
        xt = ar.alloc("xtF", [128, T], F32)
        shb = ar.alloc("shb", [128, KTD, T], BF16)
        sq = ar.alloc("sqF", [128, T], BF16)
        bmu = ar.alloc("bmuF", [128, T], BF16)
        bis = ar.alloc("bisF", [128, T], BF16)
        h2 = ar.alloc("h2F", [128, KTD, T], BF16)
        h3 = ar.alloc("h3F", [128, 22, T], BF16)
        mur = ar.alloc_at("murF", [1, T], F32, ar.offsets["h3F"])
        isr = ar.alloc_at("isrF", [1, T], F32, ar.offsets["h3F"] + 8192)
        tmpr = ar.alloc_at("tmprF", [1, T], F32, ar.offsets["h3F"] + 16384)
        sb0 = ar.offsets["shb"]
        w1a = [ar.alloc_at(f"w1a{i}", [128, KTD, 256], BF16, sb0 + i * 4096)
               for i in range(2)]
        w1g = [ar.alloc_at(f"w1g{i}", [128, KTD, 256], BF16, sb0 + 8192 + i * 4096)
               for i in range(2)]
        w2c = [ar.alloc_at(f"w2c{i}", [128, 22, 128], BF16, sb0 + 16384 + i * 5632)
               for i in range(2)]
        gt = ar.alloc_at("gtF", [128, T], BF16, sb0 + 27648)
        for kt in range(KTD):
            kx = ("xtF",)
            DMA("Y", xt, xin_hbm[kt * 128:(kt + 1) * 128, col0:col0 + T],
                [("xin_hbm", kt, s)], [kx], tag="F.x")
            if kt < 4:
                S_act(shb[:, kt, :], xt, AF.Copy, [kx], [(sk, "shb", kt)])
            else:
                memset(shb[:, kt, 0:1], 0.0, [], [(sk, "shb", kt)])
                S_act(shb[:, kt, 1:T], xt[:, 0:T - 1], AF.Copy,
                      [kx], [(sk, "shb", kt)])
            S_act(sq, shb[:, kt, :], AF.Square, [(sk, "shb", kt)], [(sk, "sq")])
            for ch in range(4):
                T_mm(PSA[0:1, ch * 512:(ch + 1) * 512], onesb[:, 0:1],
                     shb[:, kt, ch * 512:(ch + 1) * 512], kt == 0, False,
                     [(sk, "shb", kt), ("c", "ones")], [bankA(ch)], tag="F.sum",
                     sgc=True)
                T_mm(PSA[32:33, ch * 512:(ch + 1) * 512], onesb[:, 0:1],
                     sq[:, ch * 512:(ch + 1) * 512], False, kt == KTD - 1,
                     [(sk, "sq"), ("c", "ones")], [bankA(ch)], tag="F.sumsq",
                     sgc=True)
        V_ts(mur, PSA[0:1, 0:T], 1.0 / D, ALU.mult,
             [bankA(0), bankA(1), bankA(2), bankA(3)], [(sk, "mur")])
        V_ts(isr, PSA[32:33, 0:T], 1.0 / D, ALU.mult,
             [bankA(0), bankA(1), bankA(2), bankA(3)], [(sk, "isr")],
             s2=LNEPS, op1=ALU.add)
        V_tt(tmpr, mur, mur, ALU.mult, [(sk, "mur")], [(sk, "tmpr")])
        V_tt(isr, isr, tmpr, ALU.subtract,
             [(sk, "isr"), (sk, "tmpr")], [(sk, "isr")])
        S_act(isr, isr, AF.Ln, [(sk, "isr")], [(sk, "isr")])
        S_act(isr, isr, AF.Exp, [(sk, "isr")], [(sk, "isr")], scale=-0.5)
        for ch in range(4):
            T_mm(PSA[:, ch * 512:(ch + 1) * 512], onesf[0:1, :],
                 mur[0:1, ch * 512:(ch + 1) * 512], True, True,
                 [(sk, "mur"), ("c", "onesf")], [bankA(ch)], tag="F.bmu")
            T_mm(PSB[:, ch * 512:(ch + 1) * 512], onesf[0:1, :],
                 isr[0:1, ch * 512:(ch + 1) * 512], True, True,
                 [(sk, "isr"), ("c", "onesf")], [bankB(ch)], tag="F.bis")
        S_act(bmu, PSA[:, 0:T], AF.Copy,
              [bankA(0), bankA(1), bankA(2), bankA(3)], [(sk, "bmu")])
        S_act(bis, PSB[:, 0:T], AF.Copy,
              [bankB(0), bankB(1), bankB(2), bankB(3)], [(sk, "bis")])
        for kt in range(KTD):
            V_tt(h2[:, kt, :], shb[:, kt, :], bmu, ALU.subtract,
                 [(sk, "shb", kt), (sk, "bmu")], [(sk, "h2", kt)])
            V_tt(h2[:, kt, :], h2[:, kt, :], bis, ALU.mult,
                 [(sk, "h2", kt), (sk, "bis")], [(sk, "h2", kt)], eng="P")
            S_act(h2[:, kt, :], h2[:, kt, :], AF.Identity,
                  [(sk, "h2", kt), (lk, "lg"), (lk, "lb")], [(sk, "h2", kt)],
                  scale=lg_c[:, kt:kt + 1], bias=lb_c[:, kt:kt + 1])
        for mi in range(6):
            sc.link((sk, "h3", mi), [(sk, "mur"), (sk, "isr"), (sk, "tmpr")])
        shbk = [(sk, "shb", kt) for kt in range(KTD)] + [(sk, "sq")]
        for gi in range(2):
            sc.link((sk, "w1a", gi), shbk)
            sc.link((sk, "w1g", gi), shbk)
            sc.link((sk, "w2c", gi), shbk)
        for hf in range(2):
            sc.link((sk, "gt", hf), shbk)
        memset(h3[32:64, 21, :], 0.0, [], [(sk, "h3", 21)])
        memset(h3[64:128, 21, :], 0.0, [], [(sk, "h3", 21)], eng="P")
        for grp in range(11):
            mi0, gi = grp * 2, grp % 2
            tba = ((l * 2 + 0) * 11 + grp) * 128
            tbg = ((l * 2 + 1) * 11 + grp) * 128
            DMA("S", w1a[gi][:],
                w1_bf[tba:tba + 128, :].rearrange("p (a c) -> p a c", a=KTD),
                [("w1i", l, 0, grp)], [(sk, "w1a", gi)], tag="F.w1a")
            DMA("P", w1g[gi][:],
                w1_bf[tbg:tbg + 128, :].rearrange("p (a c) -> p a c", a=KTD),
                [("w1i", l, 1, grp)], [(sk, "w1g", gi)], tag="F.w1g")
            for mi_ in range(2):
                mi = mi0 + mi_
                mw = min(128, HID - mi * 128)
                for kt in range(KTD):
                    for ch in range(4):
                        T_mm(PSA[0:mw, ch * 512:(ch + 1) * 512],
                             w1a[gi][:, kt, mi_ * 128:mi_ * 128 + mw],
                             h2[:, kt, ch * 512:(ch + 1) * 512],
                             kt == 0, kt == KTD - 1,
                             [(sk, "w1a", gi), (sk, "h2", kt)],
                             [bankA(ch)], tag="F.w1am")
                        T_mm(PSB[0:mw, ch * 512:(ch + 1) * 512],
                             w1g[gi][:, kt, mi_ * 128:mi_ * 128 + mw],
                             h2[:, kt, ch * 512:(ch + 1) * 512],
                             kt == 0, kt == KTD - 1,
                             [(sk, "w1g", gi), (sk, "h2", kt)],
                             [bankB(ch)], tag="F.w1gm")
                for hf in range(2):
                    hc = slice(hf * 1024, (hf + 1) * 1024)
                    S_act(gt[0:mw, hc], PSB[0:mw, hc], AF.Gelu,
                          [bankB(2 * hf), bankB(2 * hf + 1), (lk, "b1")],
                          [(sk, "gt", hf)],
                          bias=b1g_c[0:mw, mi:mi + 1], tag="F.gelu")
                    V_stt(h3[0:mw, mi, hc], PSA[0:mw, hc],
                          b1a_c[0:mw, mi:mi + 1], gt[0:mw, hc], ALU.add,
                          ALU.mult,
                          [bankA(2 * hf), bankA(2 * hf + 1), (sk, "gt", hf),
                           (lk, "b1")],
                          [(sk, "h3", mi)], tag="F.gate")
        for mo in range(KTD):
            gi = mo % 2
            tb2 = (l * KTD + mo) * 128
            DMA(("S", "P")[gi], w2c[gi][:],
                w2_bf[tb2:tb2 + 128, :].rearrange("p (a c) -> p a c", a=22),
                [("w2i", l, mo)], [(sk, "w2c", gi)], tag="F.w2c")
            P, nm = (PSA, "PSA") if mo % 2 == 0 else (PSB, "PSB")
            for kh in range(22):
                for ch in range(4):
                    T_mm(P[:, ch * 512:(ch + 1) * 512],
                         w2c[gi][:, kh, :], h3[:, kh, ch * 512:(ch + 1) * 512],
                         kh == 0, kh == 21,
                         [(sk, "w2c", gi), (sk, "h3", kh)],
                         [(nm, ch)], tag="F.w2m")
            kx = ("xtF",)
            DMA("Y", xt, xin_hbm[mo * 128:(mo + 1) * 128, col0:col0 + T],
                [("xin_hbm", mo, s)], [kx], tag="F.w2x")
            V_stt(xt, P[:, 0:T], b2_c[:, mo:mo + 1], xt, ALU.add, ALU.add,
                  [(nm, 0), (nm, 1), (nm, 2), (nm, 3), kx, (lk, "b2")], [kx],
                  tag="F.w2add")
            DMA("Y", xin_hbm[mo * 128:(mo + 1) * 128, col0:col0 + T], xt,
                [kx], [("xin_hbm", mo, s)], tag="F.w2out")
        ar.reset()

    return emit_inner


_PROG_CACHE = {}


def kernel(L0_attn_out, L0_attn_Ww, L0_attn_Wr, L0_attn_decay, L0_attn_log_alpha,
           L0_ffn_wqkv, L0_ffn_q_scale, L0_ffn_k_scale, L0_ffn_wo, L0_ffn_ln_g,
           L0_ffn_ln_b, L0_ffn_w1, L0_ffn_b1, L0_ffn_w2, L0_ffn_b2):
    x0 = np.asarray(L0_attn_out, dtype=np.float32)
    shards = []
    for c in range(NCORES):
        sh = x0[c * SEQ:(c + 1) * SEQ]
        sh = np.ascontiguousarray(sh.transpose(2, 0, 1).reshape(D, M))
        shards.append(sh)
    weights = {
        "Ww": np.ascontiguousarray(L0_attn_Ww, np.float32),
        "Wr": np.ascontiguousarray(L0_attn_Wr, np.float32),
        "decay": np.ascontiguousarray(L0_attn_decay, np.float32).reshape(1, DN_HEADS),
        "log_alpha": np.ascontiguousarray(L0_attn_log_alpha, np.float32).reshape(1, DN_HEADS),
        "wqkv": np.ascontiguousarray(L0_ffn_wqkv, np.float32).reshape(IN_DEPTH * D, 3 * INNER),
        "q_scale": np.ascontiguousarray(L0_ffn_q_scale, np.float32),
        "k_scale": np.ascontiguousarray(L0_ffn_k_scale, np.float32),
        "wo": np.ascontiguousarray(L0_ffn_wo, np.float32).reshape(IN_DEPTH * INNER, D),
        "ln_g": np.ascontiguousarray(L0_ffn_ln_g, np.float32),
        "ln_b": np.ascontiguousarray(L0_ffn_ln_b, np.float32),
        "w1": np.ascontiguousarray(L0_ffn_w1, np.float32).reshape(IN_DEPTH * D, HID2),
        "b1": np.ascontiguousarray(L0_ffn_b1, np.float32),
        "w2": np.ascontiguousarray(L0_ffn_w2, np.float32).reshape(IN_DEPTH * HID, D),
        "b2": np.ascontiguousarray(L0_ffn_b2, np.float32),
    }
    consts = {"c_" + k: np.ascontiguousarray(v) for k, v in CONSTS.items()}

    key = "prog"
    if key not in _PROG_CACHE:
        _PROG_CACHE[key] = build_program()
    nc, _es = _PROG_CACHE[key]

    in_maps = []
    for c in range(NCORES):
        m = {"x0": shards[c]}
        m.update(weights)
        m.update(consts)
        in_maps.append(m)
    res = run_bass_kernel_spmd(nc, in_maps, core_ids=list(range(NCORES)))
    outs = []
    for c in range(NCORES):
        y = res.results[c]["out"]
        y = y.reshape(D, SEQ, T).transpose(1, 2, 0)
        outs.append(y)
    full = np.concatenate(outs, axis=0).astype(np.float32)
    if DEBUG_DUMP:
        kernel.debug_results = res.results
    return full
